# version 03 (final)
# baseline (speedup 1.0000x reference)
import contextlib
import numpy as np
import concourse.bass as bass
import concourse.mybir as mybir
from concourse.bass_utils import run_bass_kernel_spmd

F32 = mybir.dt.float32
BF16 = mybir.dt.bfloat16
AF = mybir.ActivationFunctionType
ALU = mybir.AluOpType
AX = mybir.AxisListType

NCORES = 8
N = 2048
E = 1024
TC = 2 * N
PM = 64
SP = N + PM
SPAD = 2176
TCP = 2 * SP
RMAGIC = 12582912.0
TWO_PI = 6.283185307179586

SCRATCH = {
    "xn_fm":    ([E, TC], BF16),
    "qr_fm":    ([E, TC], BF16),
    "kr_fm":    ([E, TC], BF16),
    "v_tm":     ([TC, E], BF16),
    "xg_fm":    ([64, TC], BF16),
    "attn0_tm": ([TC, E], F32),
    "xpa":      ([TC, E], F32),
    "x1":       ([TC, E], F32),
    "xn1_fm":   ([E, 2 * SPAD], BF16),
    "xin_fm_c": ([E, TCP], BF16),
    "xin_tm":   ([TCP, E], BF16),
    "q1_fm":    ([E, 2 * SPAD], BF16),
    "k1_fm":    ([E, 2 * SPAD], BF16),
    "v1_tm":    ([2 * SPAD, E], BF16),
    "A_tm":     ([TCP, E], F32),
    "h1_bf":    ([4096, TCP], BF16),
    "a_fm":     ([4096, TCP], BF16),
    "r_bf_tm":  ([TCP, E], BF16),
    "a_tm":     ([TCP, 4096], BF16),
    "dh1_tm":   ([TCP, 4096], BF16),
    "gradA":    ([1025, 4096], F32),
    "gradB":    ([4097, 1024], F32),
    "gradAr":   ([1025, 4096], F32),
    "gradBr":   ([4097, 1024], F32),
    "w1pT_bf":  ([E, 4096], BF16),
    "w2pT_bf":  ([4096, E], BF16),
    "b1p":      ([128, 32], F32),
    "b2p":      ([1, E], F32),
    "a2_fm":    ([4096, TCP], BF16),
    "om_tm":    ([TCP, E], F32),
    "xpf":      ([TC, E], F32),
}

PREP_INPUTS = {
    "x":     ([TC, E], F32),
    "cosF":  ([N, E], F32),
    "sinF":  ([N, E], F32),
    "wq":    ([E, E], F32), "wk": ([E, E], F32), "wv": ([E, E], F32),
    "wo":    ([E, E], F32),
    "wg1":   ([E, 64], F32), "wg2": ([64, E], F32),
    "gamma": ([1, E], F32), "beta": ([1, E], F32),
    "gwT":   ([E, 32], F32), "gbias": ([1, 32], F32),
    "mw_g":  ([32, 8, 128, 512], F32), "mw_u": ([32, 8, 128, 512], F32),
    "mw_d":  ([32, 4, 128, 1024], F32),
    "sw_g":  ([8, 128, 512], F32), "sw_u": ([8, 128, 512], F32),
    "sw_d":  ([4, 128, 1024], F32),
    "aw_g":  ([16, 8, 128, 256], F32), "aw_u": ([16, 8, 128, 256], F32),
    "aw_d":  ([16, 2, 128, 1024], F32),
    "mem":   ([PM, E], F32),
    "l1qT":  ([E, E], F32), "l1kT": ([E, E], F32), "l1vT": ([E, E], F32),
    "l1oT":  ([E, E], F32), "lin1T": ([E, E], F32), "lin2T": ([E, E], F32),
    "w1T":   ([E, 4096], F32), "w2T": ([4096, E], F32), "w2n": ([E, 4096], F32),
    "b1c":   ([128, 32], F32), "b2row": ([1, E], F32),
    "alpha": ([1, 1], F32), "theta": ([1, 1], F32),
    "fW1":   ([E, 4096], F32), "fW2": ([4096, E], F32),
    "fb1c":  ([128, 32], F32), "fb2row": ([1, E], F32),
    "phc":   ([128, 32], F32), "wrc": ([128, 32], F32), "wic": ([128, 32], F32),
}

PHASE_IO = {
    "p0":  ({"x"}, {"xn_fm"}),
    "p1q": ({"xn_fm", "wq", "cosF", "sinF"}, {"qr_fm"}),
    "p1k": ({"xn_fm", "wk", "cosF", "sinF"}, {"kr_fm"}),
    "p1v": ({"xn_fm", "wv", "wg1"}, {"v_tm", "xg_fm"}),
    "p2":  ({"qr_fm", "kr_fm", "v_tm"}, {"attn0_tm"}),
    "p3":  ({"attn0_tm", "xg_fm", "wg2", "gamma", "beta", "wo", "x"}, {"xpa"}),
    "p4":  ({"xn_fm", "gwT", "gbias", "mw_g", "mw_u", "mw_d", "sw_g", "sw_u",
             "sw_d", "aw_g", "aw_u", "aw_d", "xpa"}, {"x1"}),
    "p5":  ({"x1", "mem"}, {"xn1_fm", "xin_fm_c", "xin_tm"}),
    "p6a": ({"xn1_fm", "l1qT", "l1kT", "l1vT"}, {"q1_fm", "k1_fm", "v1_tm"}),
    "p6b": ({"q1_fm", "k1_fm", "v1_tm", "l1oT"}, {"A_tm"}),
    "p7":  ({"xin_fm_c", "w1T", "b1c", "w2T", "b2row", "A_tm"},
            {"h1_bf", "a_fm", "a_tm", "r_bf_tm"}),
    "p8":  ({"r_bf_tm", "w2n", "h1_bf", "xin_tm", "a_tm"},
            {"dh1_tm", "gradA", "gradB"}),
    "p9":  ({"gradA", "gradB"}, {"gradAr", "gradBr"}),
    "p10": ({"xn1_fm", "fW1", "fb1c", "phc", "wrc", "wic", "fW2", "fb2row",
             "x1"}, {"xpf"}),
    "p11": ({"gradAr", "gradBr", "w1T", "w2T", "b1c", "b2row", "alpha",
             "theta", "xin_fm_c"}, {"om_tm"}),
    "p12": ({"A_tm", "om_tm", "lin1T", "lin2T", "xpf"}, {"y"}),
}

ALL_PHASES = ["p0", "p1q", "p1k", "p1v", "p2", "p3", "p4", "p5", "p6a", "p6b",
              "p7", "p8", "p9", "p10", "p11", "p12"]


class Sem:
    def __init__(self, nc, name):
        self.h = nc.alloc_semaphore(name)
        self.v = 0
        self.base = 0

    @property
    def lv(self):
        return self.v - self.base

    def at(self, k):
        return self.base + k

    def inc(self, instr, k=1):
        instr.then_inc(self.h, k)
        self.v += k
        return self.v


class MK:
    def __init__(self, phases, fused):
        self.nc = bass.Bass("TRN2", target_bir_lowering=False, debug=False)
        self.phases = list(phases)
        self.fused = fused
        self.reads, self.writes = set(), set()
        for p in self.phases:
            r, w = PHASE_IO[p]
            self.reads |= r
            self.writes |= w
        self.in_names = []
        self.out_names = []
        self.ap = {}
        nc = self.nc
        for name in sorted(self.reads & set(PREP_INPUTS)):
            shape, dt = PREP_INPUTS[name]
            self.ap[name] = nc.dram_tensor(name, shape, dt,
                                           kind="ExternalInput").ap()
            self.in_names.append(name)
        if "y" in self.writes:
            self.ap["y"] = nc.dram_tensor("y", [TC, E], F32,
                                          kind="ExternalOutput").ap()
            self.out_names.append("y")
        for name in sorted((self.reads | self.writes) & set(SCRATCH)):
            shape, dt = SCRATCH[name]
            if fused:
                self.ap[name] = nc.dram_tensor(name, shape, dt,
                                               kind="Internal").ap()
            else:
                produced_here = name in self.writes
                consumed_here = name in self.reads
                need_in = consumed_here and not self._produced_before_use(name)
                if produced_here:
                    self.ap[name] = nc.dram_tensor(name, shape, dt,
                                                   kind="ExternalOutput").ap()
                    self.out_names.append(name)
                    assert not need_in, f"{name}: read-before-write across launch"
                elif need_in:
                    self.ap[name] = nc.dram_tensor(name, shape, dt,
                                                   kind="ExternalInput").ap()
                    self.in_names.append(name)
                else:
                    self.ap[name] = nc.dram_tensor(name, shape, dt,
                                                   kind="Internal").ap()
        self.t, self.v, self.s, self.g, self.sy = (nc.tensor, nc.vector,
                                                   nc.scalar, nc.gpsimd,
                                                   nc.sync)
        self._semid = 0
        self._uid = 0
        self._sempool = []
        self._phase_sems = None
        self._prep_common()
        for p in self.phases:
            self._phase_sems = []
            with contextlib.ExitStack() as es:
                self.es = es
                getattr(self, p)()
            self.barrier()
            self._sempool.extend(self._phase_sems)
            self._phase_sems = None

    def _produced_before_use(self, name):
        seen_w = False
        for p in self.phases:
            r, w = PHASE_IO[p]
            if name in r and not seen_w:
                return False
            if name in w:
                seen_w = True
        return True

    def sem(self, tag="s"):
        if self._phase_sems is None:
            self._semid += 1
            return Sem(self.nc, f"sm{self._semid}_{tag}")
        if self._sempool:
            s = self._sempool.pop()
            s.base = s.v
        else:
            self._semid += 1
            s = Sem(self.nc, f"sm{self._semid}_{tag}")
            s.base = s.v
        self._phase_sems.append(s)
        return s

    def uid(self, tag):
        self._uid += 1
        return f"{tag}{self._uid}"

    def sbuf(self, tag, shape, dt=F32):
        return self.es.enter_context(self.nc.sbuf_tensor(self.uid(tag), shape, dt))

    def psum(self, tag, shape, dt=F32):
        return self.es.enter_context(self.nc.psum_tensor(self.uid(tag), shape, dt))

    def barrier(self):
        self.nc.all_engine_barrier()

    def _prep_common(self):
        with contextlib.ExitStack() as es:
            pass
        nc, g = self.nc, self.g
        self.ident_bf = nc.alloc_sbuf_tensor("ident_bf", [128, 128], BF16)
        self.ident_f = nc.alloc_sbuf_tensor("ident_f", [128, 128], F32)
        for tt in (self.ident_bf, self.ident_f):
            g.memset(tt[:], 0.0)
            g.affine_select(out=tt[:], in_=tt[:], compare_op=ALU.not_equal,
                            fill=1.0, base=0, pattern=[[-1, 128]],
                            channel_multiplier=1)
        self.trilT_f = nc.alloc_sbuf_tensor("trilT_f", [128, 128], F32)
        g.memset(self.trilT_f[:], 1.0)
        g.affine_select(out=self.trilT_f[:], in_=self.trilT_f[:],
                        compare_op=ALU.is_ge, fill=0.0, base=0,
                        pattern=[[1, 128]], channel_multiplier=-1)
        self.mdiag = nc.alloc_sbuf_tensor("mdiag", [128, 128], F32)
        g.memset(self.mdiag[:], 0.0)
        g.affine_select(out=self.mdiag[:], in_=self.mdiag[:],
                        compare_op=ALU.is_ge, fill=-1e30, base=0,
                        pattern=[[-1, 128]], channel_multiplier=1)
        self.mfar = nc.alloc_sbuf_tensor("mfar", [128, 128], F32)
        g.memset(self.mfar[:], 0.0)
        g.affine_select(out=self.mfar[:], in_=self.mfar[:],
                        compare_op=ALU.is_ge, fill=-1e30, base=-1,
                        pattern=[[1, 128]], channel_multiplier=-1)
        self.barrier()

    def tm2fm(self, src_sb, n128, dstf, ptA, ptB, s_tp, s_fr, evac=None):
        t, s = self.t, self.s
        for c in range(n128):
            pt = (ptA, ptB)[s_tp.lv % 2]
            if s_tp.lv >= 2:
                t.wait_ge(s_fr.h, s_fr.at(s_tp.lv - 1))
            ins = self.t.transpose(pt[:], src_sb[:, 128 * c:128 * (c + 1)],
                                   self.ident_bf[:])
            s_tp.inc(ins, 1)
            s.wait_ge(s_tp.h, s_tp.v)
            if evac is None:
                ins = s.copy(dstf(c), pt[:])
            else:
                ins = evac(dstf(c), pt)
            s_fr.inc(ins, 1)

    def p0(self):
        v, s, t, sy = self.v, self.s, self.t, self.sy
        xin, xout = self.ap["x"], self.ap["xn_fm"]
        xs = [self.sbuf("p0x", [128, E]) for _ in range(2)]
        sq = self.sbuf("p0sq", [128, E])
        stats = [self.sbuf("p0st", [128, 8]) for _ in range(2)]
        xnb = [self.sbuf("p0nb", [128, E], BF16) for _ in range(2)]
        fm = self.sbuf("p0fm", [128, 8, TC], BF16)
        ptA = self.psum("p0pA", [128, 128], BF16)
        ptB = self.psum("p0pB", [128, 128], BF16)
        s_ld, s_stat, s_ev, s_tp, s_fr, s_st = (self.sem("ld"), self.sem("st8"),
                                                self.sem("ev"), self.sem("tp"),
                                                self.sem("fr"), self.sem("out"))
        ntile = TC // 128
        for i in range(ntile):
            buf = xs[i % 2]
            xb = xnb[i % 2]
            stat = stats[i % 2]
            if i >= 2:
                sy.wait_ge(s_ev.h, s_ev.at(i - 1))
            sy.dma_start(out=buf[:], in_=xin[128 * i:128 * (i + 1), :]) \
                .then_inc(s_ld.h, 16)
            s_ld.v += 16
            v.wait_ge(s_ld.h, s_ld.v)
            s_stat.inc(v.reduce_sum(stat[:, 0:1], buf[:], axis=AX.X))
            v.wait_ge(s_stat.h, s_stat.v)
            s_stat.inc(v.tensor_scalar_mul(stat[:, 1:2], stat[:, 0:1], -1.0 / E))
            s.wait_ge(s_stat.h, s_stat.v)
            s_stat.inc(s.activation(sq[:], buf[:], AF.Square, bias=stat[:, 1:2],
                                    accum_out=stat[:, 2:3]))
            v.wait_ge(s_stat.h, s_stat.v)
            s_stat.inc(v.tensor_scalar(stat[:, 3:4], stat[:, 2:3], 1.0 / E,
                                       1e-5, ALU.mult, ALU.add))
            s.wait_ge(s_stat.h, s_stat.v)
            s_stat.inc(s.activation(stat[:, 4:5], stat[:, 3:4], AF.Sqrt))
            v.wait_ge(s_stat.h, s_stat.v)
            s_stat.inc(v.reciprocal(stat[:, 5:6], stat[:, 4:5]))
            v.wait_ge(s_stat.h, s_stat.v)
            if i >= 2:
                v.wait_ge(s_tp.h, s_tp.at(8 * (i - 1)))
            s_ev.inc(v.tensor_scalar(xb[:], buf[:], stat[:, 1:2], stat[:, 5:6],
                                     ALU.add, ALU.mult))
            t.wait_ge(s_ev.h, s_ev.v)
            self.tm2fm(xb[:], 8, lambda c, i=i: fm[:, c, 128 * i:128 * (i + 1)],
                       ptA, ptB, s_tp, s_fr)
        sy.wait_ge(s_fr.h, s_fr.v)
        for c in range(8):
            sy.dma_start(out=xout[128 * c:128 * (c + 1), :], in_=fm[:, c, :]) \
                .then_inc(s_st.h, 16)
            s_st.v += 16
        sy.wait_ge(s_st.h, s_st.v)

    def _load_fm(self, name, tag):
        src = self.ap[name]
        cols = src.shape[1]
        fm = self.sbuf(tag, [128, 8, cols], BF16)
        s_l = self.sem(tag + "l")
        for c in range(8):
            self.sy.dma_start(out=fm[:, c, :], in_=src[128 * c:128 * (c + 1), :]) \
                .then_inc(s_l.h, 16)
            s_l.v += 16
        return fm, s_l

    def _load_w_cast(self, name, tag, nchunk=8, cols=None):
        src = self.ap[name]
        cols = cols if cols is not None else src.shape[-1]
        w = self.sbuf(tag, [128, nchunk, cols], BF16)
        s_l = self.sem(tag + "l")
        for c in range(nchunk):
            self.g.dma_start(out=w[:, c, :], in_=src[128 * c:128 * (c + 1), :]) \
                .then_inc(s_l.h, 16)
            s_l.v += 16
        return w, s_l

    def _proj_rot(self, wname, outname):
        v, s, t, g, sy = self.v, self.s, self.t, self.g, self.sy
        fm, s_lfm = self._load_fm("xn_fm", "xn")
        w, s_lw = self._load_w_cast(wname, "w")
        out_hbm = self.ap[outname]
        ofm = self.sbuf("ofm", [128, 8, TC], BF16)
        pp = [self.psum("pp", [128, 512]) for _ in range(4)]
        ptA = self.psum("ptA", [128, 128], BF16)
        ptB = self.psum("ptB", [128, 128], BF16)
        qs = [self.sbuf("qs", [128, E]) for _ in range(2)]
        qsw = [self.sbuf("qsw", [128, E]) for _ in range(2)]
        cosT = [self.sbuf("cosT", [128, E]) for _ in range(2)]
        sinT = [self.sbuf("sinT", [128, E]) for _ in range(2)]
        m1 = [self.sbuf("m1", [128, E]) for _ in range(2)]
        m2 = [self.sbuf("m2", [128, E]) for _ in range(2)]
        rot = [self.sbuf("rot", [128, E], BF16) for _ in range(2)]
        s_mm = self.sem("mm")
        s_ev = self.sem("ev")
        s_ld = self.sem("ld")
        s_sw = self.sem("sw")
        s_g = self.sem("g")
        s_rot = self.sem("rt")
        s_tp = self.sem("tp")
        s_fr = self.sem("fr")
        s_st = self.sem("st")
        t.wait_ge(s_lfm.h, s_lfm.v)
        t.wait_ge(s_lw.h, s_lw.v)
        for i in range(TC // 128):
            par = i % 2
            pos = 128 * (i % 16)
            if i >= 2:
                sy.wait_ge(s_rot.h, s_rot.at(i - 1))
            sy.dma_start(out=cosT[par][:], in_=self.ap["cosF"][pos:pos + 128, :]) \
                .then_inc(s_ld.h, 16)
            sy.dma_start(out=sinT[par][:], in_=self.ap["sinF"][pos:pos + 128, :]) \
                .then_inc(s_ld.h, 16)
            s_ld.v += 32
            pb = pp[2 * par], pp[2 * par + 1]
            if i >= 2:
                t.wait_ge(s_ev.h, s_ev.at(2 * i - 2))
            for c in range(8):
                for hlf in range(2):
                    ins = t.matmul(pb[hlf][:], fm[:, c, 128 * i:128 * (i + 1)],
                                   w[:, c, 512 * hlf:512 * (hlf + 1)],
                                   start=(c == 0), stop=(c == 7))
            s_mm.inc(ins, 1)
            s.wait_ge(s_mm.h, s_mm.v)
            if i >= 2:
                s.wait_ge(s_rot.h, s_rot.at(i - 1))
            ins = s.activation(qs[par][:, 0:512], pb[0][:], AF.Silu)
            ins = s.activation(qs[par][:, 512:1024], pb[1][:], AF.Silu)
            s_ev.inc(ins, 2)
            v.wait_ge(s_ev.h, s_ev.v)
            if i >= 2:
                v.wait_ge(s_g.h, s_g.at(i - 1))
            qv = qs[par][:].rearrange("p (h two d) -> p h two d", two=2, d=32)
            sv = qsw[par][:].rearrange("p (h two d) -> p h two d", two=2, d=32)
            v.tensor_copy(sv[:, :, 0, :], qv[:, :, 1, :])
            ins = v.tensor_copy(sv[:, :, 1, :], qv[:, :, 0, :])
            s_sw.inc(ins, 1)
            v.wait_ge(s_ld.h, s_ld.v)
            v.tensor_mul(m1[par][:], qs[par][:], cosT[par][:])
            g.wait_ge(s_sw.h, s_sw.v)
            g.wait_ge(s_ld.h, s_ld.v)
            if i >= 2:
                g.wait_ge(s_rot.h, s_rot.at(i - 1))
            ins = g.tensor_mul(m2[par][:], qsw[par][:], sinT[par][:])
            s_g.inc(ins, 1)
            v.wait_ge(s_g.h, s_g.v)
            if i >= 2:
                v.wait_ge(s_tp.h, s_tp.at(8 * (i - 1)))
            ins = v.tensor_add(rot[par][:], m1[par][:], m2[par][:])
            s_rot.inc(ins, 1)
            t.wait_ge(s_rot.h, s_rot.v)
            self.tm2fm(rot[par][:], 8,
                       lambda c, i=i: ofm[:, c, 128 * i:128 * (i + 1)],
                       ptA, ptB, s_tp, s_fr)
        sy.wait_ge(s_fr.h, s_fr.v)
        for c in range(8):
            sy.dma_start(out=out_hbm[128 * c:128 * (c + 1), :], in_=ofm[:, c, :]) \
                .then_inc(s_st.h, 16)
            s_st.v += 16
        sy.wait_ge(s_st.h, s_st.v)

    def p1q(self):
        self._proj_rot("wq", "qr_fm")

    def p1k(self):
        self._proj_rot("wk", "kr_fm")

    def p1v(self):
        v, s, t, g, sy = self.v, self.s, self.t, self.g, self.sy
        fm, s_lfm = self._load_fm("xn_fm", "xn")
        w, s_lw = self._load_w_cast("wv", "wv")
        wg1 = self.sbuf("wg1b", [128, 8, 64], BF16)
        s_lw2 = self.sem("wg1l")
        for c in range(8):
            g.dma_start(out=wg1[:, c, :],
                        in_=self.ap["wg1"][128 * c:128 * (c + 1), :]) \
                .then_inc(s_lw2.h, 16)
            s_lw2.v += 16
        vout = self.ap["v_tm"]
        xgout = self.ap["xg_fm"]
        pp = [self.psum("pv", [128, 512]) for _ in range(4)]
        pxg = [self.psum("pxg", [64, 512]) for _ in range(2)]
        vb = [self.sbuf("vb", [128, E], BF16) for _ in range(2)]
        xg = self.sbuf("xg", [64, 8, 512], BF16)
        s_mm = self.sem("mm")
        s_ev = self.sem("ev")
        s_st = self.sem("st")
        s_gmm = self.sem("gmm")
        s_gev = self.sem("gev")
        t.wait_ge(s_lfm.h, s_lfm.v)
        t.wait_ge(s_lw.h, s_lw.v)
        t.wait_ge(s_lw2.h, s_lw2.v)
        for i in range(TC // 128):
            par = i % 2
            pb = pp[2 * par], pp[2 * par + 1]
            if i >= 2:
                t.wait_ge(s_ev.h, s_ev.at(i - 1))
            for c in range(8):
                for hlf in range(2):
                    ins = t.matmul(pb[hlf][:], fm[:, c, 128 * i:128 * (i + 1)],
                                   w[:, c, 512 * hlf:512 * (hlf + 1)],
                                   start=(c == 0), stop=(c == 7))
            s_mm.inc(ins, 1)
            s.wait_ge(s_mm.h, s_mm.v)
            if i >= 2:
                s.wait_ge(s_st.h, s_st.at(16 * (i - 1)))
            s.copy(vb[par][:, 0:512], pb[0][:])
            ins = s.copy(vb[par][:, 512:1024], pb[1][:])
            s_ev.inc(ins, 1)
            sy.wait_ge(s_ev.h, s_ev.v)
            sy.dma_start(out=vout[128 * i:128 * (i + 1), :], in_=vb[par][:]) \
                .then_inc(s_st.h, 16)
            s_st.v += 16
        for jt in range(8):
            pg = pxg[jt % 2]
            if jt >= 2:
                t.wait_ge(s_gev.h, s_gev.at(jt - 1))
            for c in range(8):
                ins = t.matmul(pg[:], wg1[:, c, :],
                               fm[:, c, 512 * jt:512 * (jt + 1)],
                               start=(c == 0), stop=(c == 7))
            s_gmm.inc(ins, 1)
            s.wait_ge(s_gmm.h, s_gmm.v)
            ins = s.copy(xg[:, jt, :], pg[:])
            s_gev.inc(ins, 1)
        sy.wait_ge(s_gev.h, s_gev.v)
        sy.dma_start(out=xgout[:, :], in_=xg[:].rearrange("p a b -> p (a b)")) \
            .then_inc(s_st.h, 16)
        s_st.v += 16
        sy.wait_ge(s_st.h, s_st.v)

    def p2(self):
        v, s, t, g, sy = self.v, self.s, self.t, self.g, self.sy
        aout = self.ap["attn0_tm"]
        s_lq = self.sem("lq")
        s_emm = self.sem("emm")
        s_eev = self.sem("eev")
        s_pv = self.sem("pv")
        s_oev = self.sem("oev")
        s_tp = self.sem("tp")
        s_tev = self.sem("tev")
        s_st = self.sem("st")
        qsb = self.sbuf("q2", [128, 8, N], BF16)
        ksb = self.sbuf("k2", [128, 8, N], BF16)
        vsb = self.sbuf("v2", [128, 16, E], BF16)
        esb = [self.sbuf("e2", [128, 512], BF16) for _ in range(2)]
        osb = self.sbuf("o2", [64, 512])
        at = self.sbuf("at", [128, 4, E])
        pe_ = [self.psum("pe2", [128, 512]) for _ in range(2)]
        po = self.psum("po2", [64, 512])
        ptf = [self.psum("ptf", [128, 128]) for _ in range(2)]
        for seq in range(2):
            if seq == 1:
                sy.wait_ge(s_st.h, s_st.v)
                sy.wait_ge(s_eev.h, s_eev.v)
                sy.wait_ge(s_tev.h, s_tev.v)
            for c in range(8):
                sy.dma_start(out=qsb[:, c, :],
                             in_=self.ap["qr_fm"][128 * c:128 * (c + 1),
                                                  N * seq:N * (seq + 1)]) \
                    .then_inc(s_lq.h, 16)
                sy.dma_start(out=ksb[:, c, :],
                             in_=self.ap["kr_fm"][128 * c:128 * (c + 1),
                                                  N * seq:N * (seq + 1)]) \
                    .then_inc(s_lq.h, 16)
                s_lq.v += 32
            for i in range(16):
                sy.dma_start(out=vsb[:, i, :],
                             in_=self.ap["v_tm"][N * seq + 128 * i:
                                                 N * seq + 128 * (i + 1), :]) \
                    .then_inc(s_lq.h, 16)
                s_lq.v += 16
            t.wait_ge(s_lq.h, s_lq.v)
            v.wait_ge(s_lq.h, s_lq.v)
            s_ebz = self.sem("ebz")
            at_guard = [0]
            for qg in range(4):
                for h in range(16):
                    ch, rb = h // 2, 64 * (h % 2)
                    nkc = 4 * qg + 4
                    t.wait_ge(s_oev.h, s_oev.v)
                    for kc in range(nkc):
                        pb = pe_[kc % 2]
                        eb = esb[kc % 2]
                        if s_emm.lv >= 2:
                            t.wait_ge(s_eev.h, s_eev.at(s_emm.lv - 1))
                        ins = t.matmul(pb[:],
                                       ksb[rb:rb + 64, ch, 128 * kc:128 * (kc + 1)],
                                       qsb[rb:rb + 64, ch, 512 * qg:512 * (qg + 1)],
                                       start=True, stop=True)
                        s_emm.inc(ins, 1)
                        v.wait_ge(s_emm.h, s_emm.v)
                        if s_emm.lv >= 2:
                            v.wait_ge(s_pv.h, s_pv.at(s_emm.lv - 1))
                        sdiag = kc - 4 * qg
                        nz = 0
                        if sdiag < 0:
                            ins = v.tensor_copy(eb[:], pb[:])
                        else:
                            if sdiag > 0:
                                g.wait_ge(s_emm.h, s_emm.v)
                                if s_emm.lv >= 2:
                                    g.wait_ge(s_pv.h, s_pv.at(s_emm.lv - 1))
                                insz = g.memset(eb[:, 0:128 * sdiag], 0.0)
                                s_ebz.inc(insz, 1)
                                nz = 1
                            ins = v.tensor_mul(
                                eb[:, 128 * sdiag:128 * (sdiag + 1)],
                                pb[:, 128 * sdiag:128 * (sdiag + 1)],
                                self.trilT_f[:])
                            if sdiag < 3:
                                ins = v.tensor_copy(
                                    eb[:, 128 * (sdiag + 1):512],
                                    pb[:, 128 * (sdiag + 1):512])
                        s_eev.inc(ins, 1)
                        t.wait_ge(s_eev.h, s_eev.v)
                        if nz:
                            t.wait_ge(s_ebz.h, s_ebz.v)
                        ins = t.matmul(po[:], vsb[:, kc, 64 * h:64 * (h + 1)],
                                       eb[:], start=(kc == 0),
                                       stop=(kc == nkc - 1))
                        s_pv.inc(ins, 1)
                    s.wait_ge(s_pv.h, s_pv.v)
                    s.wait_ge(s_tp.h, s_tp.v)
                    ins = s.copy(osb[:], po[:])
                    s_oev.inc(ins, 1)
                    t.wait_ge(s_oev.h, s_oev.v)
                    for j in range(4):
                        pt = ptf[s_tp.lv % 2]
                        if s_tp.lv >= 2:
                            t.wait_ge(s_tev.h, s_tev.at(s_tp.lv - 1))
                        ins = t.transpose(pt[:, 0:64],
                                          osb[:, 128 * j:128 * (j + 1)],
                                          self.ident_f[0:64, 0:64])
                        s_tp.inc(ins, 1)
                        s.wait_ge(s_tp.h, s_tp.v)
                        s.wait_ge(s_st.h, at_guard[0])
                        ins = s.copy(at[:, j, 64 * h:64 * (h + 1)], pt[:, 0:64])
                        s_tev.inc(ins, 1)
                sy.wait_ge(s_tev.h, s_tev.v)
                for j in range(4):
                    q0 = N * seq + 512 * qg + 128 * j
                    sy.dma_start(out=aout[q0:q0 + 128, :], in_=at[:, j, :]) \
                        .then_inc(s_st.h, 16)
                    s_st.v += 16
                at_guard[0] = s_st.v
        sy.wait_ge(s_st.h, s_st.v)

    def p3(self):
        v, s, t, g, sy = self.v, self.s, self.t, self.g, self.sy
        wg2 = self.sbuf("wg2", [64, 1, E], BF16)
        s_lw1 = self.sem("wg2l")
        g.dma_start(out=wg2[:, 0, :], in_=self.ap["wg2"][:, :]) \
            .then_inc(s_lw1.h, 16)
        s_lw1.v += 16
        wo, s_lw2 = self._load_w_cast("wo", "wo")
        gab = self.sbuf("gab", [128, E])
        beb = self.sbuf("beb", [128, E])
        s_lc = self.sem("lc")
        self.g.dma_start(out=gab[:], in_=self.ap["gamma"].partition_broadcast(128)) \
            .then_inc(s_lc.h, 16)
        self.g.dma_start(out=beb[:], in_=self.ap["beta"].partition_broadcast(128)) \
            .then_inc(s_lc.h, 16)
        s_lc.v += 32
        a0 = [self.sbuf("a0", [128, E]) for _ in range(2)]
        xt = [self.sbuf("xt", [128, E]) for _ in range(2)]
        xgs = [self.sbuf("xgs", [64, 128], BF16) for _ in range(2)]
        gates = [self.sbuf("gate", [128, E]) for _ in range(2)]
        ggs = [self.sbuf("gg", [128, E]) for _ in range(2)]
        tmps = [self.sbuf("tmp", [128, E]) for _ in range(2)]
        gnb = [self.sbuf("gnb", [128, E], BF16) for _ in range(2)]
        gnf = self.sbuf("gnf", [128, 8, 128], BF16)
        stats = [self.sbuf("p3st", [128, 8]) for _ in range(2)]
        sq = self.sbuf("p3sq", [128, E])
        xpas = [self.sbuf("xpas", [128, E]) for _ in range(2)]
        pg = [self.psum("pg3", [128, 512]) for _ in range(2)]
        po = [self.psum("po3", [128, 512]) for _ in range(2)]
        ptA = self.psum("ptA3", [128, 128], BF16)
        ptB = self.psum("ptB3", [128, 128], BF16)
        s_ld = self.sem("ld")
        s_gmm = self.sem("gmm")
        s_gev = self.sem("gev")
        s_ln = self.sem("ln")
        s_tp = self.sem("tp")
        s_fr = self.sem("fr")
        s_omm = self.sem("omm")
        s_oev = self.sem("oev")
        s_st = self.sem("st")
        t.wait_ge(s_lw1.h, s_lw1.v)
        t.wait_ge(s_lw2.h, s_lw2.v)
        v.wait_ge(s_lc.h, s_lc.v)
        for i in range(TC // 128):
            par = i % 2
            stat = stats[par]
            gate, gg, tmp = gates[par], ggs[par], tmps[par]
            if i >= 2:
                sy.wait_ge(s_oev.h, s_oev.at(i - 1))
            sy.dma_start(out=a0[par][:],
                         in_=self.ap["attn0_tm"][128 * i:128 * (i + 1), :]) \
                .then_inc(s_ld.h, 16)
            sy.dma_start(out=xt[par][:],
                         in_=self.ap["x"][128 * i:128 * (i + 1), :]) \
                .then_inc(s_ld.h, 16)
            sy.dma_start(out=xgs[par][:],
                         in_=self.ap["xg_fm"][:, 128 * i:128 * (i + 1)]) \
                .then_inc(s_ld.h, 16)
            s_ld.v += 48
            t.wait_ge(s_ld.h, s_ld.v)
            if i >= 1:
                t.wait_ge(s_gev.h, s_gev.at(2 * i))
            for hlf in range(2):
                ins = t.matmul(pg[hlf][:], xgs[par][:],
                               wg2[:, 0, 512 * hlf:512 * (hlf + 1)],
                               start=True, stop=True)
                s_gmm.inc(ins, 1)
            s.wait_ge(s_gmm.h, s_gmm.v)
            if i >= 2:
                s.wait_ge(s_ln.h, s_ln.at(8 * (i - 1)))
            ins = s.activation(gate[:, 0:512], pg[0][:], AF.Sigmoid)
            ins = s.activation(gate[:, 512:1024], pg[1][:], AF.Sigmoid)
            s_gev.inc(ins, 2)
            v.wait_ge(s_gev.h, s_gev.v)
            v.wait_ge(s_ld.h, s_ld.v)
            ins = v.tensor_mul(gg[:], a0[par][:], gate[:])
            s_ln.inc(ins, 1)
            v.wait_ge(s_ln.h, s_ln.v)
            s_ln.inc(v.reduce_sum(stat[:, 0:1], gg[:], axis=AX.X))
            v.wait_ge(s_ln.h, s_ln.v)
            s_ln.inc(v.tensor_scalar_mul(stat[:, 1:2], stat[:, 0:1], -1.0 / E))
            s.wait_ge(s_ln.h, s_ln.v)
            s_ln.inc(s.activation(sq[:], gg[:], AF.Square, bias=stat[:, 1:2],
                                  accum_out=stat[:, 2:3]))
            v.wait_ge(s_ln.h, s_ln.v)
            s_ln.inc(v.tensor_scalar(stat[:, 3:4], stat[:, 2:3], 1.0 / E, 1e-5,
                                     ALU.mult, ALU.add))
            s.wait_ge(s_ln.h, s_ln.v)
            s_ln.inc(s.activation(stat[:, 4:5], stat[:, 3:4], AF.Sqrt))
            v.wait_ge(s_ln.h, s_ln.v)
            s_ln.inc(v.reciprocal(stat[:, 5:6], stat[:, 4:5]))
            v.wait_ge(s_ln.h, s_ln.v)
            v.tensor_scalar(tmp[:], gg[:], stat[:, 1:2], stat[:, 5:6],
                            ALU.add, ALU.mult)
            v.tensor_mul(tmp[:], tmp[:], gab[:])
            if i >= 2:
                v.wait_ge(s_tp.h, s_tp.at(8 * (i - 1)))
            ins = v.tensor_add(gnb[par][:], tmp[:], beb[:])
            s_ln.inc(ins, 1)
            t.wait_ge(s_ln.h, s_ln.v)
            s.wait_ge(s_omm.h, s_omm.v)
            self.tm2fm(gnb[par][:], 8, lambda c: gnf[:, c, :], ptA, ptB,
                       s_tp, s_fr)
            t.wait_ge(s_fr.h, s_fr.v)
            if i >= 1:
                t.wait_ge(s_oev.h, s_oev.at(i))
            for hlf in range(2):
                for c in range(8):
                    ins = t.matmul(po[hlf][:], gnf[:, c, :],
                                   wo[:, c, 512 * hlf:512 * (hlf + 1)],
                                   start=(c == 0), stop=(c == 7))
                s_omm.inc(ins, 1)
            v.wait_ge(s_omm.h, s_omm.v)
            if i >= 2:
                v.wait_ge(s_st.h, s_st.at(16 * (i - 1)))
            v.tensor_add(xpas[par][:, 0:512], po[0][:], xt[par][:, 0:512])
            ins = v.tensor_add(xpas[par][:, 512:1024], po[1][:],
                               xt[par][:, 512:1024])
            s_oev.inc(ins, 1)
            sy.wait_ge(s_oev.h, s_oev.v)
            sy.dma_start(out=self.ap["xpa"][128 * i:128 * (i + 1), :],
                         in_=xpas[par][:]).then_inc(s_st.h, 16)
            s_st.v += 16
        sy.wait_ge(s_st.h, s_st.v)


    def p4(self):
        v, s, t, g, sy = self.v, self.s, self.t, self.g, self.sy
        nc = self.nc
        mwg_bf = nc.dram_tensor("mwg_bf", [32, 128, 4096], BF16).ap()
        mwu_bf = nc.dram_tensor("mwu_bf", [32, 128, 4096], BF16).ap()
        mwd_bf = nc.dram_tensor("mwd_bf", [32, 128, 4096], BF16).ap()
        swg_bf = nc.dram_tensor("swg_bf", [128, 4096], BF16).ap()
        swu_bf = nc.dram_tensor("swu_bf", [128, 4096], BF16).ap()
        swd_bf = nc.dram_tensor("swd_bf", [128, 4096], BF16).ap()
        awg_bf = nc.dram_tensor("awg_bf", [16, 128, 2048], BF16).ap()
        awu_bf = nc.dram_tensor("awu_bf", [16, 128, 2048], BF16).ap()
        awd_bf = nc.dram_tensor("awd_bf", [16, 128, 2048], BF16).ap()
        cwT = self.sbuf("cwT", [32, TC], BF16)
        acwT = self.sbuf("acwT", [16, TC], BF16)
        sel32 = self.sbuf("sel32", [32, 32, 128], BF16)
        sel16 = self.sbuf("sel16", [16, 16, 128], BF16)
        for selt, ne in ((sel32, 32), (sel16, 16)):
            g.memset(selt[:], 0.0)
            g.affine_select(out=selt[:], in_=selt[:], compare_op=ALU.not_equal,
                            fill=1.0, base=0, pattern=[[-1, ne], [0, 128]],
                            channel_multiplier=1)
        s_selc = self.sem("selc")
        s_selc.inc(g.nop(), 1)

        sub = contextlib.ExitStack()
        outer_es = self.es
        self.es = sub
        cast_sb = [self.sbuf("cst", [128, 4096], BF16) for _ in range(2)]
        s_ci = self.sem("ci")
        s_co = self.sem("co")
        jobs = []
        for e in range(32):
            jobs += [(self.ap["mw_g"][e], mwg_bf[e]),
                     (self.ap["mw_u"][e], mwu_bf[e]),
                     (self.ap["mw_d"][e], mwd_bf[e])]
        jobs += [(self.ap["sw_g"], swg_bf), (self.ap["sw_u"], swu_bf),
                 (self.ap["sw_d"], swd_bf)]
        for l in range(16):
            jobs += [(self.ap["aw_g"][l], awg_bf[l]),
                     (self.ap["aw_u"][l], awu_bf[l]),
                     (self.ap["aw_d"][l], awd_bf[l])]
        for j, (src, dst) in enumerate(jobs):
            buf = cast_sb[j % 2]
            nch, _, isz = src.shape
            ncols = nch * isz
            src3 = src.rearrange("c p i -> p c i")
            bufv = buf[:, 0:ncols].rearrange("p (c i) -> p c i", i=isz)
            if j >= 2:
                g.wait_ge(s_co.h, s_co.at(16 * (j - 1)))
            g.dma_start(out=bufv, in_=src3).then_inc(s_ci.h, 16)
            s_ci.v += 16
            sy.wait_ge(s_ci.h, s_ci.v)
            sy.dma_start(out=dst, in_=buf[:, 0:ncols]).then_inc(s_co.h, 16)
            s_co.v += 16
        fm, s_lfm = self._load_fm("xn_fm", "xn")
        gwT, s_lg1 = self._load_w_cast("gwT", "gwT")
        gbB = self.sbuf("gbB", [128, 32])
        s_lg2 = self.sem("gb")
        g.dma_start(out=gbB[:], in_=self.ap["gbias"].partition_broadcast(128)) \
            .then_inc(s_lg2.h, 16)
        s_lg2.v += 16
        scs = self.sbuf("scs", [128, 32])
        scb = self.sbuf("scb", [128, 32])
        t8g = self.sbuf("t8g", [128, 8])
        gs4 = self.sbuf("gs4", [128, 8])
        t8x = self.sbuf("t8x", [128, 8])
        g.memset(gs4[:], -1e30)
        s_selc.inc(g.nop(), 1)
        msk = self.sbuf("msk", [128, 32])
        gbb = self.sbuf("gbb", [128, 8])
        t8m = self.sbuf("t8m", [128, 8])
        selm = self.sbuf("selm", [128, 32])
        sel1 = self.sbuf("sel1", [128, 32])
        wacc = self.sbuf("wacc", [128, 32])
        rstat = self.sbuf("rstat", [128, 4])
        cwb = self.sbuf("cwb", [128, 32], BF16)
        acwb = self.sbuf("acwb", [128, 16], BF16)
        ppr = self.psum("ppr", [128, 512])
        ppt = self.psum("ppt", [128, 128])
        s_rmm = self.sem("rmm")
        s_rv = self.sem("rv")
        s_rs = self.sem("rs")
        s_rtp = self.sem("rtp")
        s_rte = self.sem("rte")
        t.wait_ge(s_lfm.h, s_lfm.v)
        t.wait_ge(s_lg1.h, s_lg1.v)
        v.wait_ge(s_lg2.h, s_lg2.v)
        ins = v.memset(ppt[:], 0.0)
        s_rte.inc(ins, 1)
        t.wait_ge(s_rte.h, s_rte.v)

        def vstep(ins):
            s_rv.inc(ins, 1)
            v.wait_ge(s_rv.h, s_rv.v)

        NRV = 25
        for i in range(TC // 128):
            t.wait_ge(s_rs.h, s_rs.v)
            for c in range(8):
                ins = t.matmul(ppr[:, 0:32], fm[:, c, 128 * i:128 * (i + 1)],
                               gwT[:, c, :], start=(c == 0), stop=(c == 7))
            s_rmm.inc(ins, 1)
            s.wait_ge(s_rmm.h, s_rmm.v)
            if i >= 1:
                s.wait_ge(s_rv.h, s_rv.at(NRV * i - 7))
            ins = s.activation(scs[:], ppr[:, 0:32], AF.Sigmoid)
            s_rs.inc(ins, 1)
            v.wait_ge(s_rs.h, s_rs.v)
            if i == 0:
                v.wait_ge(s_selc.h, 2)
            vstep(v.tensor_add(scb[:], scs[:], gbB[:]))
            for gi in range(4):
                vstep(v.max(t8g[:], scb[:, 8 * gi:8 * (gi + 1)]))
                vstep(v.tensor_add(gs4[:, gi:gi + 1], t8g[:, 0:1],
                                   t8g[:, 1:2]))
            vstep(v.max(t8x[:], gs4[:]))
            for gi in range(4):
                vstep(v.tensor_single_scalar(gbb[:, gi:gi + 1],
                                             gs4[:, gi:gi + 1],
                                             t8x[:, 1:2], ALU.is_ge))
            vstep(v.tensor_scalar(gbb[:, 4:8], gbb[:, 0:4], 1e30, -1e30,
                                  ALU.mult, ALU.add))
            s.wait_ge(s_rv.h, s_rv.v)
            for gi in range(4):
                ins = s.activation(msk[:, 8 * gi:8 * (gi + 1)],
                                   scb[:, 8 * gi:8 * (gi + 1)], AF.Identity,
                                   bias=gbb[:, 4 + gi:5 + gi])
            s_rs.inc(ins, 1)
            v.wait_ge(s_rs.h, s_rs.v)
            vstep(v.max(t8m[:], msk[:]))
            vstep(v.tensor_single_scalar(selm[:], msk[:], t8m[:, 5:6],
                                         ALU.is_ge))
            vstep(v.tensor_mul(wacc[:], selm[:], scs[:]))
            vstep(v.reduce_sum(rstat[:, 0:1], wacc[:], axis=AX.X))
            vstep(v.tensor_scalar_add(rstat[:, 1:2], rstat[:, 0:1], 1e-20))
            vstep(v.reciprocal(rstat[:, 2:3], rstat[:, 1:2]))
            if i >= 1:
                v.wait_ge(s_rtp.h, s_rtp.at(2 * i))
            vstep(v.tensor_scalar(cwb[:], wacc[:], rstat[:, 2:3], 0.2,
                                  ALU.mult, ALU.mult))
            vstep(v.tensor_single_scalar(sel1[:], msk[:], t8m[:, 0:1],
                                         ALU.is_ge))
            s2 = sel1[:].rearrange("p (e two) -> p e two", two=2)
            vstep(v.tensor_tensor(acwb[:], s2[:, :, 0], s2[:, :, 1],
                                  ALU.max))
            ins = v.tensor_scalar_mul(acwb[:], acwb[:], 0.1)
            s_rv.inc(ins, 1)
            v.wait_ge(s_rv.h, s_rv.v)
            import os as _os
            if _os.environ.get("MKDBG3") and i < 2:
                if not hasattr(self, "_dbg3sb"):
                    self._dbg3sb = self.sbuf("dbg3sb", [128, 2, 192])
                    self._dbg3sem = self.sem("dbg3")
                db3 = self._dbg3sb
                v.tensor_copy(db3[:, i, 0:32], cwb[:])
                v.tensor_copy(db3[:, i, 32:64], scs[:])
                v.tensor_copy(db3[:, i, 64:96], msk[:])
                v.tensor_copy(db3[:, i, 96:128], selm[:])
                v.tensor_copy(db3[:, i, 128:136], t8m[:])
                v.tensor_copy(db3[:, i, 136:140], rstat[:])
                v.tensor_copy(db3[:, i, 140:148], gbb[:])
                v.tensor_copy(db3[:, i, 148:156], t8x[:])
                v.tensor_copy(db3[:, i, 156:164], gs4[:])
                ins = v.tensor_copy(db3[:, i, 164:192], wacc[:, 0:28])
                self._dbg3sem.inc(ins, 1)
            t.wait_ge(s_rv.h, s_rv.v)
            t.wait_ge(s_selc.h, 1)
            if i >= 1:
                t.wait_ge(s_rte.h, s_rte.at(2 * i))
            ins = t.transpose(ppt[0:32, 0:64].bitcast(BF16), cwb[:],
                              self.ident_bf[:])
            s_rtp.inc(ins, 1)
            v.wait_ge(s_rtp.h, s_rtp.v)
            ins = v.tensor_copy(cwT[:, 128 * i:128 * (i + 1)],
                                ppt[0:32, 0:64].bitcast(BF16))
            s_rte.inc(ins, 1)
            t.wait_ge(s_rte.h, s_rte.v)
            ins = t.transpose(ppt[0:16, 64:128].bitcast(BF16), acwb[:],
                              self.ident_bf[:])
            s_rtp.inc(ins, 1)
            v.wait_ge(s_rtp.h, s_rtp.v)
            ins = v.tensor_copy(acwT[:, 128 * i:128 * (i + 1)],
                                ppt[0:16, 64:128].bitcast(BF16))
            s_rte.inc(ins, 1)
        import os
        if os.environ.get("MKDBG3"):
            dbg3 = self.nc.dram_tensor("dbg3", [128, 2, 192], F32,
                                       kind="ExternalOutput").ap()
            self.out_names.append("dbg3")
            sy.wait_ge(self._dbg3sem.h, self._dbg3sem.v)
            sy.dma_start(out=dbg3, in_=self._dbg3sb[:]) \
                .then_inc(self._dbg3sem.h, 16)
            self._dbg3sem.v += 16
            sy.wait_ge(self._dbg3sem.h, self._dbg3sem.v)
        if os.environ.get("MKDBG2"):
            dbg2 = self.nc.dram_tensor("dbg2", [128, 2, 126], F32,
                                       kind="ExternalOutput").ap()
            self.out_names.append("dbg2")
            sy.wait_ge(self._dbg2sem.h, self._dbg2sem.v)
            sy.dma_start(out=dbg2, in_=self._dbg2sb[:]) \
                .then_inc(self._dbg2sem.h, 16)
            self._dbg2sem.v += 16
            sy.wait_ge(self._dbg2sem.h, self._dbg2sem.v)
        if os.environ.get("MKDBG"):
            dbg = self.nc.dram_tensor("dbgcw", [48, TC], F32,
                                      kind="ExternalOutput").ap()
            self.out_names.append("dbgcw")
            dbf = self.sbuf("dbf", [48, TC])
            s_dbg = self.sem("dbg")
            v.wait_ge(s_rte.h, s_rte.v)
            v.tensor_copy(dbf[0:32, :], cwT[:])
            ins = v.tensor_copy(dbf[32:48, :], acwT[:])
            s_dbg.inc(ins, 1)
            sy.wait_ge(s_dbg.h, s_dbg.v)
            sy.dma_start(out=dbg, in_=dbf[:]).then_inc(s_dbg.h, 16)
            s_dbg.v += 16
            sy.wait_ge(s_dbg.h, s_dbg.v)
        sy.wait_ge(s_co.h, s_co.v)
        s.wait_ge(s_rte.h, s_rte.v)
        self.barrier()
        self.es = outer_es
        sub.close()

        wbuf = [{"g": self.sbuf("wg", [128, 4096], BF16),
                 "u": self.sbuf("wu", [128, 4096], BF16),
                 "d": self.sbuf("wd", [128, 4096], BF16)} for _ in range(2)]
        xnt = self.sbuf("xnt", [128, 8, 256], BF16)
        cwBa = self.sbuf("cwBa", [128, 32, 256])
        acwBa = self.sbuf("acwBa", [128, 16, 256])
        sg_sb = self.sbuf("sg", [128, 256])
        t1_sb = self.sbuf("t1", [128, 256])
        m_sb = self.sbuf("m", [128, 4, 256], BF16)
        xpat = [self.sbuf("xpat", [128, 2, 1024]) for _ in range(2)]
        x1t = [self.sbuf("x1t", [128, 2, 1024]) for _ in range(2)]
        pr_ = [self.psum("pr4", [128, 1024]) for _ in range(2)]
        pg_ = self.psum("pg4", [128, 1024])
        pu_ = self.psum("pu4", [128, 1024])
        s_wl = self.sem("wl")
        s_xl = self.sem("xl")
        s_bc = self.sem("bc")
        s_bce = self.sem("bce")
        s_gu = self.sem("gu")
        s_me = self.sem("me")
        s_dn = self.sem("dn")
        s_oe = self.sem("oe")
        s_os = self.sem("os")
        njob = 49
        ntile = TC // 256

        def _issue_w(jglob):
            slot = jglob % 2
            jj = jglob % njob
            if jj < 32:
                srcs = (mwg_bf[jj], mwu_bf[jj], mwd_bf[jj])
            elif jj == 32:
                srcs = (swg_bf, swu_bf, swd_bf)
            else:
                srcs = (awg_bf[jj - 33], awu_bf[jj - 33], awd_bf[jj - 33])
            for k, key in enumerate(("g", "u", "d")):
                ncols = srcs[k].shape[-1]
                sy.dma_start(out=wbuf[slot][key][:, 0:ncols], in_=srcs[k]) \
                    .then_inc(s_wl.h, 16)
                s_wl.v += 16

        _issue_w(0)
        for ti in range(ntile):
            par = ti % 2
            if ti >= 1:
                sy.wait_ge(s_gu.h, s_gu.at(2 * njob * ti))
            for c in range(8):
                sy.dma_start(out=xnt[:, c, :],
                             in_=self.ap["xn_fm"][128 * c:128 * (c + 1),
                                                  256 * ti:256 * (ti + 1)]) \
                    .then_inc(s_xl.h, 16)
                s_xl.v += 16
            if ti >= 2:
                sy.wait_ge(s_oe.h, s_oe.at(ti - 1))
            for th in range(2):
                sy.dma_start(out=xpat[par][:, th, :],
                             in_=self.ap["xpa"][256 * ti + 128 * th:
                                                256 * ti + 128 * (th + 1), :]) \
                    .then_inc(s_xl.h, 16)
                s_xl.v += 16
            bjobs = [(cwBa[:, e, :], sel32[:, e, :], cwT) for e in range(32)] \
                + [(acwBa[:, l, :], sel16[:, l, :], acwT) for l in range(16)]
            t.wait_ge(s_xl.h, s_xl.v)
            t.wait_ge(s_me.h, s_me.v)
            s.wait_ge(s_me.h, s_me.v)
            base_bc, base_bce = s_bc.v, s_bce.v
            for k, (dst, sel, srcT) in enumerate(bjobs):
                reg = k % 2
                if k >= 2:
                    t.wait_ge(s_bce.h, base_bce + k - 1)
                ins = t.matmul(pu_[:, 512 * reg:512 * reg + 256], sel,
                               srcT[:, 256 * ti:256 * (ti + 1)],
                               start=True, stop=True)
                s_bc.inc(ins, 1)
                s.wait_ge(s_bc.h, s_bc.v)
                ins = s.copy(dst, pu_[:, 512 * reg:512 * reg + 256])
                s_bce.inc(ins, 1)
            first = True
            for jj in range(njob):
                jglob = ti * njob + jj
                slot = jglob % 2
                if jglob + 1 < ntile * njob:
                    if jglob >= 1:
                        sy.wait_ge(s_dn.h, s_dn.v)
                    _issue_w(jglob + 1)
                nic = 4 if jj < 33 else 2
                isz = 512 if jj < 33 else 256
                t.wait_ge(s_wl.h, s_wl.at(48 * (jglob + 1)))
                t.wait_ge(s_me.h, s_me.v)
                if jj == 0:
                    t.wait_ge(s_bce.h, s_bce.v)
                wgv = wbuf[slot]["g"][:].rearrange("p (c i) -> p c i", i=isz)
                wuv = wbuf[slot]["u"][:].rearrange("p (c i) -> p c i", i=isz)
                wdv = wbuf[slot]["d"][:].rearrange("p (c i) -> p c i", i=1024)
                for ic in range(nic):
                    for c in range(8):
                        ins = t.matmul(pg_[:, 256 * ic:256 * (ic + 1)],
                                       wgv[:, c, 128 * ic:128 * (ic + 1)],
                                       xnt[:, c, :], start=(c == 0),
                                       stop=(c == 7))
                s_gu.inc(ins, 1)
                for ic in range(nic):
                    for c in range(8):
                        ins = t.matmul(pu_[:, 256 * ic:256 * (ic + 1)],
                                       wuv[:, c, 128 * ic:128 * (ic + 1)],
                                       xnt[:, c, :], start=(c == 0),
                                       stop=(c == 7))
                s_gu.inc(ins, 1)
                s.wait_ge(s_gu.h, s_gu.v)
                v.wait_ge(s_dn.h, s_dn.v)
                for ic in range(nic):
                    s.wait_ge(s_me.h, s_me.v)
                    ins = s.activation(sg_sb[:], pg_[:, 256 * ic:256 * (ic + 1)],
                                       AF.Silu)
                    s_me.inc(ins, 1)
                    v.wait_ge(s_me.h, s_me.v)
                    v.tensor_mul(t1_sb[:], sg_sb[:],
                                 pu_[:, 256 * ic:256 * (ic + 1)])
                    if jj < 32:
                        ins = v.tensor_mul(m_sb[:, ic, :], t1_sb[:],
                                           cwBa[:, jj, :])
                    elif jj == 32:
                        ins = v.tensor_copy(m_sb[:, ic, :], t1_sb[:])
                    else:
                        ins = v.tensor_mul(m_sb[:, ic, :], t1_sb[:],
                                           acwBa[:, jj - 33, :])
                    s_me.inc(ins, 1)
                t.wait_ge(s_me.h, s_me.v)
                if first:
                    t.wait_ge(s_oe.h, s_oe.v)
                for ic in range(nic):
                    for th in range(2):
                        for dh in range(2):
                            last = (jj == njob - 1 and ic == nic - 1)
                            ins = t.matmul(
                                pr_[th][:, 512 * dh:512 * (dh + 1)],
                                m_sb[:, ic, 128 * th:128 * (th + 1)],
                                wdv[:, ic, 512 * dh:512 * (dh + 1)],
                                start=(first and ic == 0), stop=last)
                s_dn.inc(ins, 1)
                first = False
            v.wait_ge(s_dn.h, s_dn.v)
            if ti >= 2:
                v.wait_ge(s_os.h, s_os.at(32 * (ti - 1)))
            v.tensor_add(x1t[par][:, 0, :], pr_[0][:], xpat[par][:, 0, :])
            ins = v.tensor_add(x1t[par][:, 1, :], pr_[1][:], xpat[par][:, 1, :])
            s_oe.inc(ins, 1)
            sy.wait_ge(s_oe.h, s_oe.v)
            for th in range(2):
                sy.dma_start(out=self.ap["x1"][256 * ti + 128 * th:
                                               256 * ti + 128 * (th + 1), :],
                             in_=x1t[par][:, th, :]).then_inc(s_os.h, 16)
                s_os.v += 16
        sy.wait_ge(s_os.h, s_os.v)


    def p5(self):
        v, s, t, g, sy = self.v, self.s, self.t, self.g, self.sy
        x1 = self.ap["x1"]
        xs = [self.sbuf("p5x", [128, E]) for _ in range(2)]
        sq = self.sbuf("p5sq", [128, E])
        stats = [self.sbuf("p5st", [128, 8]) for _ in range(2)]
        xnb = [self.sbuf("p5nb", [128, E], BF16) for _ in range(2)]
        fm = self.sbuf("p5fm", [128, 8, 2 * SPAD], BF16)
        memb = self.sbuf("p5mem", [64, E], BF16)
        ptA = self.psum("p5pA", [128, 128], BF16)
        ptB = self.psum("p5pB", [128, 128], BF16)
        s_ld = self.sem("ld")
        s_stat = self.sem("st8")
        s_ev = self.sem("ev")
        s_tp = self.sem("tp")
        s_fr = self.sem("fr")
        s_st = self.sem("out")
        s_mm = self.sem("mm")
        g.dma_start(out=memb[:], in_=self.ap["mem"][:, :]).then_inc(s_mm.h, 16)
        s_mm.v += 16
        sy.wait_ge(s_mm.h, s_mm.v)
        for seq in range(2):
            sy.dma_start(out=self.ap["xin_tm"][SP * seq:SP * seq + PM, :],
                         in_=memb[:]).then_inc(s_st.h, 16)
            s_st.v += 16
        t.wait_ge(s_mm.h, s_mm.v)
        for seq in range(2):
            for c in range(8):
                pt = (ptA, ptB)[s_tp.lv % 2]
                if s_tp.lv >= 2:
                    t.wait_ge(s_fr.h, s_fr.at(s_tp.lv - 1))
                ins = self.t.transpose(pt[:, 0:64],
                                       memb[:, 128 * c:128 * (c + 1)],
                                       self.ident_bf[0:64, 0:64])
                s_tp.inc(ins, 1)
                s.wait_ge(s_tp.h, s_tp.v)
                ins = s.copy(fm[:, c, SPAD * seq:SPAD * seq + PM], pt[:, 0:64])
                s_fr.inc(ins, 1)
        for c in range(8):
            for seq in range(2):
                ins = g.memset(fm[:, c, SPAD * seq + SP:SPAD * (seq + 1)], 0.0)
        s_mm.inc(g.nop(), 1)
        ntile = TC // 128
        for i in range(ntile):
            buf, xb, stat = xs[i % 2], xnb[i % 2], stats[i % 2]
            seq, j = i // 16, i % 16
            if i >= 2:
                sy.wait_ge(s_ev.h, s_ev.at(i - 1))
            sy.dma_start(out=buf[:], in_=x1[128 * i:128 * (i + 1), :]) \
                .then_inc(s_ld.h, 16)
            s_ld.v += 16
            v.wait_ge(s_ld.h, s_ld.v)
            s_stat.inc(v.reduce_sum(stat[:, 0:1], buf[:], axis=AX.X))
            v.wait_ge(s_stat.h, s_stat.v)
            s_stat.inc(v.tensor_scalar_mul(stat[:, 1:2], stat[:, 0:1], -1.0 / E))
            s.wait_ge(s_stat.h, s_stat.v)
            s_stat.inc(s.activation(sq[:], buf[:], AF.Square, bias=stat[:, 1:2],
                                    accum_out=stat[:, 2:3]))
            v.wait_ge(s_stat.h, s_stat.v)
            s_stat.inc(v.tensor_scalar(stat[:, 3:4], stat[:, 2:3], 1.0 / E,
                                       1e-5, ALU.mult, ALU.add))
            s.wait_ge(s_stat.h, s_stat.v)
            s_stat.inc(s.activation(stat[:, 4:5], stat[:, 3:4], AF.Sqrt))
            v.wait_ge(s_stat.h, s_stat.v)
            s_stat.inc(v.reciprocal(stat[:, 5:6], stat[:, 4:5]))
            v.wait_ge(s_stat.h, s_stat.v)
            if i >= 2:
                v.wait_ge(s_tp.h, s_tp.at(16 + 8 * (i - 1)))
            s_ev.inc(v.tensor_scalar(xb[:], buf[:], stat[:, 1:2], stat[:, 5:6],
                                     ALU.add, ALU.mult))
            sy.wait_ge(s_ev.h, s_ev.v)
            r0 = SP * seq + PM + 128 * j
            sy.dma_start(out=self.ap["xin_tm"][r0:r0 + 128, :], in_=xb[:]) \
                .then_inc(s_st.h, 16)
            s_st.v += 16
            t.wait_ge(s_ev.h, s_ev.v)
            col0 = SPAD * seq + PM + 128 * j
            self.tm2fm(xb[:], 8, lambda c, col0=col0: fm[:, c, col0:col0 + 128],
                       ptA, ptB, s_tp, s_fr)
        sy.wait_ge(s_fr.h, s_fr.v)
        sy.wait_ge(s_mm.h, s_mm.v)
        for c in range(8):
            sy.dma_start(out=self.ap["xn1_fm"][128 * c:128 * (c + 1), :],
                         in_=fm[:, c, :]).then_inc(s_st.h, 16)
            s_st.v += 16
        sy.wait_ge(s_st.h, s_st.v)
        for c in range(8):
            for seq in range(2):
                sy.dma_start(
                    out=self.ap["xin_fm_c"][128 * c:128 * (c + 1),
                                            SP * seq:SP * (seq + 1)],
                    in_=self.ap["xn1_fm"][128 * c:128 * (c + 1),
                                          SPAD * seq:SPAD * seq + SP]) \
                    .then_inc(s_st.h, 16)
                s_st.v += 16
        sy.wait_ge(s_st.h, s_st.v)

    def _proj_plain(self, wname, outname, out_fmt):
        v, s, t, g, sy = self.v, self.s, self.t, self.g, self.sy
        fm, s_lfm = self._load_fm("xn1_fm", "xn1")
        w, s_lw = self._load_w_cast(wname, "w")
        out_hbm = self.ap[outname]
        ntile = 2 * SPAD // 128
        pp = [self.psum("pp", [128, 512]) for _ in range(4)]
        s_mm = self.sem("mm")
        s_ev = self.sem("ev")
        s_st = self.sem("st")
        t.wait_ge(s_lfm.h, s_lfm.v)
        t.wait_ge(s_lw.h, s_lw.v)
        if out_fmt == "tm":
            vb = [self.sbuf("vb", [128, E], BF16) for _ in range(2)]
            for i in range(ntile):
                par = i % 2
                pb = pp[2 * par], pp[2 * par + 1]
                if i >= 2:
                    t.wait_ge(s_ev.h, s_ev.at(i - 1))
                for c in range(8):
                    for hlf in range(2):
                        ins = t.matmul(pb[hlf][:],
                                       fm[:, c, 128 * i:128 * (i + 1)],
                                       w[:, c, 512 * hlf:512 * (hlf + 1)],
                                       start=(c == 0), stop=(c == 7))
                s_mm.inc(ins, 1)
                s.wait_ge(s_mm.h, s_mm.v)
                if i >= 2:
                    s.wait_ge(s_st.h, s_st.at(16 * (i - 1)))
                s.copy(vb[par][:, 0:512], pb[0][:])
                ins = s.copy(vb[par][:, 512:1024], pb[1][:])
                s_ev.inc(ins, 1)
                sy.wait_ge(s_ev.h, s_ev.v)
                sy.dma_start(out=out_hbm[128 * i:128 * (i + 1), :],
                             in_=vb[par][:]).then_inc(s_st.h, 16)
                s_st.v += 16
            sy.wait_ge(s_st.h, s_st.v)
        else:
            ofm = self.sbuf("ofm", [128, 8, 2 * SPAD], BF16)
            qb = [self.sbuf("qb", [128, E], BF16) for _ in range(2)]
            ptA = self.psum("ptA", [128, 128], BF16)
            ptB = self.psum("ptB", [128, 128], BF16)
            s_tp = self.sem("tp")
            s_fr = self.sem("fr")
            for i in range(ntile):
                par = i % 2
                pb = pp[2 * par], pp[2 * par + 1]
                if i >= 2:
                    t.wait_ge(s_ev.h, s_ev.at(i - 1))
                for c in range(8):
                    for hlf in range(2):
                        ins = t.matmul(pb[hlf][:],
                                       fm[:, c, 128 * i:128 * (i + 1)],
                                       w[:, c, 512 * hlf:512 * (hlf + 1)],
                                       start=(c == 0), stop=(c == 7))
                s_mm.inc(ins, 1)
                s.wait_ge(s_mm.h, s_mm.v)
                if i >= 2:
                    s.wait_ge(s_tp.h, s_tp.at(8 * (i - 1)))
                s.copy(qb[par][:, 0:512], pb[0][:])
                ins = s.copy(qb[par][:, 512:1024], pb[1][:])
                s_ev.inc(ins, 1)
                t.wait_ge(s_ev.h, s_ev.v)
                self.tm2fm(qb[par][:], 8,
                           lambda c, i=i: ofm[:, c, 128 * i:128 * (i + 1)],
                           ptA, ptB, s_tp, s_fr)
            sy.wait_ge(s_fr.h, s_fr.v)
            for c in range(8):
                sy.dma_start(out=out_hbm[128 * c:128 * (c + 1), :],
                             in_=ofm[:, c, :]).then_inc(s_st.h, 16)
                s_st.v += 16
            sy.wait_ge(s_st.h, s_st.v)

    def p6a(self):
        sub = contextlib.ExitStack()
        outer = self.es
        for wname, outname, fmt in (("l1qT", "q1_fm", "fm"),
                                    ("l1kT", "k1_fm", "fm"),
                                    ("l1vT", "v1_tm", "tm")):
            self.es = sub
            self._proj_plain(wname, outname, fmt)
            self.barrier()
            self.es = outer
            sub.close()
            sub = contextlib.ExitStack()

    def p6b(self):
        v, s, t, g, sy = self.v, self.s, self.t, self.g, self.sy
        wo, s_lwo = self._load_w_cast("l1oT", "woT")
        qsb = self.sbuf("q6", [128, 8, SPAD], BF16)
        ksb = self.sbuf("k6", [128, 8, SPAD], BF16)
        vsb = self.sbuf("v6", [128, 17, E], BF16)
        lsb = self.sbuf("l6", [128, 640])
        esb = self.sbuf("e6", [128, 640], BF16)
        esc = self.sbuf("esc", [128, 640], BF16)
        attnT = self.sbuf("aT6", [128, 5, 128], BF16)
        preA = self.sbuf("pA6", [128, 8, 128], BF16)
        stat = self.sbuf("st6", [128, 8])
        asb = [self.sbuf("asb", [128, E]) for _ in range(2)]
        pa = self.psum("pa6", [128, 512])
        pb = self.psum("pb6", [128, 128])
        pv0 = self.psum("pv06", [128, 128])
        pv1 = self.psum("pv16", [128, 128])
        pA = self.psum("pA6p", [128, 1024])
        ptA = self.psum("ptA6", [128, 128], BF16)
        ptB = self.psum("ptB6", [128, 128], BF16)
        s_lq = self.sem("lq")
        s_lmm = self.sem("lmm")
        s_lev = self.sem("lev")
        s_sm = self.sem("sm")
        s_tp = self.sem("tp")
        s_te = self.sem("te")
        s_pv = self.sem("pv")
        s_pe = self.sem("pe")
        s_am = self.sem("am")
        s_ae = self.sem("ae")
        s_st = self.sem("st")
        t.wait_ge(s_lwo.h, s_lwo.v)
        for seq in range(2):
            if seq == 1:
                sy.wait_ge(s_st.h, s_st.v)
            for c in range(8):
                sy.dma_start(out=qsb[:, c, :],
                             in_=self.ap["q1_fm"][128 * c:128 * (c + 1),
                                                  SPAD * seq:SPAD * (seq + 1)]) \
                    .then_inc(s_lq.h, 16)
                sy.dma_start(out=ksb[:, c, :],
                             in_=self.ap["k1_fm"][128 * c:128 * (c + 1),
                                                  SPAD * seq:SPAD * (seq + 1)]) \
                    .then_inc(s_lq.h, 16)
                s_lq.v += 32
            for i in range(17):
                sy.dma_start(out=vsb[:, i, :],
                             in_=self.ap["v1_tm"][SPAD * seq + 128 * i:
                                                  SPAD * seq + 128 * (i + 1), :]) \
                    .then_inc(s_lq.h, 16)
                s_lq.v += 16
            t.wait_ge(s_lq.h, s_lq.v)
            v.wait_ge(s_lq.h, s_lq.v)
            for qt in range(17):
                kts = list(range(max(0, qt - 4), qt + 1))
                nk = len(kts)
                for h in range(4):
                    t.wait_ge(s_lev.h, s_lev.v)
                    npa = nk - 1
                    for dc in range(2):
                        c = 2 * h + dc
                        if npa > 0:
                            ins = t.matmul(pa[:, 0:128 * npa],
                                           qsb[:, c, 128 * qt:128 * (qt + 1)],
                                           ksb[:, c, 128 * kts[0]:
                                               128 * (kts[0] + npa)],
                                           start=(dc == 0), stop=(dc == 1))
                        ins = t.matmul(pb[:],
                                       qsb[:, c, 128 * qt:128 * (qt + 1)],
                                       ksb[:, c, 128 * qt:128 * (qt + 1)],
                                       start=(dc == 0), stop=(dc == 1))
                    s_lmm.inc(ins, 1)
                    v.wait_ge(s_lmm.h, s_lmm.v)
                    if s_sm.lv >= 1:
                        v.wait_ge(s_sm.h, s_sm.v)
                    for ki in range(npa):
                        src = pa[:, 128 * ki:128 * (ki + 1)]
                        dst = lsb[:, 128 * ki:128 * (ki + 1)]
                        if qt >= 4 and ki == 0:
                            v.tensor_add(dst, src, self.mfar[:])
                        else:
                            v.tensor_copy(dst, src)
                    ins = v.tensor_add(lsb[:, 128 * npa:128 * nk], pb[:],
                                       self.mdiag[:])
                    s_lev.inc(ins, 1)
                    v.wait_ge(s_lev.h, s_lev.v)
                    ins = v.tensor_reduce(stat[:, 0:1], lsb[:, 0:128 * nk],
                                          axis=AX.X, op=ALU.max, negate=True)
                    s_sm.inc(ins, 1)
                    v.wait_ge(s_sm.h, s_sm.v)
                    ins = v.tensor_scalar_mul(stat[:, 1:2], stat[:, 0:1],
                                              1.0 / 16.0)
                    s_sm.inc(ins, 1)
                    s.wait_ge(s_sm.h, s_sm.v)
                    ins = s.activation(esb[:, 0:128 * nk], lsb[:, 0:128 * nk],
                                       AF.Exp, bias=stat[:, 1:2],
                                       scale=1.0 / 16.0,
                                       accum_out=stat[:, 2:3])
                    s_sm.inc(ins, 1)
                    v.wait_ge(s_sm.h, s_sm.v)
                    ins = v.reciprocal(stat[:, 3:4], stat[:, 2:3])
                    s_sm.inc(ins, 1)
                    v.wait_ge(s_sm.h, s_sm.v)
                    if s_tp.lv >= 1:
                        v.wait_ge(s_tp.h, s_tp.v)
                    ins = v.tensor_scalar_mul(esc[:, 0:128 * nk],
                                              esb[:, 0:128 * nk], stat[:, 3:4])
                    s_sm.inc(ins, 1)
                    t.wait_ge(s_sm.h, s_sm.v)
                    for ki in range(nk):
                        pt = (ptA, ptB)[s_tp.lv % 2]
                        if s_tp.lv >= 2:
                            t.wait_ge(s_te.h, s_te.at(s_tp.lv - 1))
                        ins = t.transpose(pt[:],
                                          esc[:, 128 * ki:128 * (ki + 1)],
                                          self.ident_bf[:])
                        s_tp.inc(ins, 1)
                        s.wait_ge(s_tp.h, s_tp.v)
                        if s_pv.lv >= 1 and ki < 2:
                            s.wait_ge(s_pv.h, s_pv.v)
                        ins = s.copy(attnT[:, ki, :], pt[:])
                        s_te.inc(ins, 1)
                    t.wait_ge(s_te.h, s_te.v)
                    if s_pe.lv >= 1:
                        t.wait_ge(s_pe.h, s_pe.v)
                    for ik, kt in enumerate(kts):
                        for dh2 in range(2):
                            c = 2 * h + dh2
                            ins = t.matmul(
                                (pv0, pv1)[dh2][:],
                                vsb[:, kt, 256 * h + 128 * dh2:
                                    256 * h + 128 * (dh2 + 1)],
                                attnT[:, ik, :],
                                start=(ik == 0), stop=(ik == nk - 1))
                    s_pv.inc(ins, 1)
                    s.wait_ge(s_pv.h, s_pv.v)
                    s.wait_ge(s_am.h, s_am.v)
                    s.copy(preA[:, 2 * h, :], pv0[:])
                    ins = s.copy(preA[:, 2 * h + 1, :], pv1[:])
                    s_pe.inc(ins, 1)
                t.wait_ge(s_pe.h, s_pe.v)
                t.wait_ge(s_ae.h, s_ae.v)
                for hlf in range(2):
                    for c in range(8):
                        ins = t.matmul(pA[:, 512 * hlf:512 * (hlf + 1)],
                                       preA[:, c, :],
                                       wo[:, c, 512 * hlf:512 * (hlf + 1)],
                                       start=(c == 0), stop=(c == 7))
                s_am.inc(ins, 1)
                v.wait_ge(s_am.h, s_am.v)
                par = qt % 2
                if s_st.lv >= 32:
                    v.wait_ge(s_st.h, s_st.v - 16)
                ins = v.tensor_copy(asb[par][:], pA[:])
                s_ae.inc(ins, 1)
                sy.wait_ge(s_ae.h, s_ae.v)
                nrow = 128 if qt < 16 else 64
                r0 = SP * seq + 128 * qt
                sy.dma_start(out=self.ap["A_tm"][r0:r0 + nrow, :],
                             in_=asb[par][0:nrow, :]).then_inc(s_st.h, 16)
                s_st.v += 16
        sy.wait_ge(s_st.h, s_st.v)

    def p7(self):
        v, s, t, g, sy = self.v, self.s, self.t, self.g, self.sy
        w1, s_l1 = self._load_w_cast("w1T", "w1T")
        w2 = self.sbuf("w2T", [128, 32, 1024], BF16)
        s_l2 = self.sem("w2l")
        for c in range(32):
            g.dma_start(out=w2[:, c, :],
                        in_=self.ap["w2T"][128 * c:128 * (c + 1), :]) \
                .then_inc(s_l2.h, 16)
            s_l2.v += 16
        b1c = self.sbuf("b1c", [128, 32])
        b2B = self.sbuf("b2B", [128, E])
        s_lb = self.sem("lb")
        sy.dma_start(out=b1c[:], in_=self.ap["b1c"][:, :]).then_inc(s_lb.h, 16)
        g.dma_start(out=b2B[:], in_=self.ap["b2row"].partition_broadcast(128)) \
            .then_inc(s_lb.h, 16)
        s_lb.v += 32
        xin = [self.sbuf("xin", [128, 8, 256], BF16) for _ in range(2)]
        atile = [self.sbuf("at7", [128, 256], BF16) for _ in range(2)]
        htile = [self.sbuf("ht7", [128, 256], BF16) for _ in range(2)]
        atm = self.sbuf("atm7", [128, 2, 4096], BF16)
        Ars = [self.sbuf("Ar7", [128, 2, E]) for _ in range(2)]
        rf = self.sbuf("rf7", [128, E])
        rb = [self.sbuf("rb7", [128, 2, E], BF16) for _ in range(2)]
        ph_ = [self.psum("ph7", [128, 256]) for _ in range(2)]
        pp2 = [self.psum("pp7", [128, 1024]) for _ in range(2)]
        ptA = self.psum("ptA7", [128, 128], BF16)
        ptB = self.psum("ptB7", [128, 128], BF16)
        s_xl = self.sem("xl")
        s_mm = self.sem("mm")
        s_ev = self.sem("ev")
        s_hs = self.sem("hs")
        s_tp = self.sem("tp")
        s_fr = self.sem("fr")
        s_pm = self.sem("pm")
        s_re = self.sem("re")
        s_rs = self.sem("rs")
        t.wait_ge(s_l1.h, s_l1.v)
        t.wait_ge(s_l2.h, s_l2.v)
        s.wait_ge(s_lb.h, s_lb.v)
        v.wait_ge(s_lb.h, s_lb.v)
        ntg = TCP // 256 + 1
        for tg in range(ntg):
            S = 256 if tg < 16 else 128
            t0 = 256 * tg
            par = tg % 2
            nth = S // 128
            if tg >= 2:
                sy.wait_ge(s_mm.h, s_mm.at(32 * (tg - 1)))
            for c in range(8):
                sy.dma_start(out=xin[par][:, c, 0:S],
                             in_=self.ap["xin_fm_c"][128 * c:128 * (c + 1),
                                                     t0:t0 + S]) \
                    .then_inc(s_xl.h, 16)
                s_xl.v += 16
            if tg >= 2:
                sy.wait_ge(s_re.h, s_re.at(nth * (tg - 1)))
            for th in range(nth):
                sy.dma_start(out=Ars[par][:, th, :],
                             in_=self.ap["A_tm"][t0 + 128 * th:
                                                 t0 + 128 * (th + 1), :]) \
                    .then_inc(s_xl.h, 16)
                s_xl.v += 16
            t.wait_ge(s_xl.h, s_xl.v)
            for fc in range(32):
                fp = fc % 2
                ph = ph_[fp]
                if s_mm.lv >= 2:
                    t.wait_ge(s_ev.h, s_ev.at(2 * s_mm.lv - 2))
                for c in range(8):
                    ins = t.matmul(ph[:, 0:S], w1[:, c, 128 * fc:128 * (fc + 1)],
                                   xin[par][:, c, 0:S], start=(c == 0),
                                   stop=(c == 7))
                s_mm.inc(ins, 1)
                s.wait_ge(s_mm.h, s_mm.v)
                if s_ev.lv >= 4:
                    s.wait_ge(s_hs.h, s_hs.at(32 * (s_ev.lv // 2 - 1)))
                    s.wait_ge(s_pm.h, s_pm.at(s_ev.lv // 2 - 1))
                ins = s.activation(atile[fp][:, 0:S], ph[:, 0:S], AF.Silu,
                                   bias=b1c[:, fc:fc + 1])
                s_ev.inc(ins, 1)
                ins = s.activation(htile[fp][:, 0:S], ph[:, 0:S], AF.Identity,
                                   bias=b1c[:, fc:fc + 1])
                s_ev.inc(ins, 1)
                sy.wait_ge(s_ev.h, s_ev.v)
                sy.dma_start(out=self.ap["a_fm"][128 * fc:128 * (fc + 1),
                                                 t0:t0 + S],
                             in_=atile[fp][:, 0:S]).then_inc(s_hs.h, 16)
                sy.dma_start(out=self.ap["h1_bf"][128 * fc:128 * (fc + 1),
                                                  t0:t0 + S],
                             in_=htile[fp][:, 0:S]).then_inc(s_hs.h, 16)
                s_hs.v += 32
                t.wait_ge(s_ev.h, s_ev.v)
                for th in range(nth):
                    pt = (ptA, ptB)[s_tp.lv % 2]
                    if s_tp.lv >= 2:
                        t.wait_ge(s_fr.h, s_fr.at(s_tp.lv - 1))
                    ins = t.transpose(pt[:],
                                      atile[fp][:, 128 * th:128 * (th + 1)],
                                      self.ident_bf[:])
                    s_tp.inc(ins, 1)
                    s.wait_ge(s_tp.h, s_tp.v)
                    if fc == 0:
                        s.wait_ge(s_rs.h, s_rs.v)
                    ins = s.copy(atm[:, th, 128 * fc:128 * (fc + 1)], pt[:])
                    s_fr.inc(ins, 1)
                t.wait_ge(s_fr.h, s_fr.v)
                if fc == 0:
                    t.wait_ge(s_re.h, s_re.v)
                for th in range(nth):
                    for dh in range(2):
                        ins = t.matmul(pp2[th][:, 512 * dh:512 * (dh + 1)],
                                       atile[fp][:, 128 * th:128 * (th + 1)],
                                       w2[:, fc, 512 * dh:512 * (dh + 1)],
                                       start=(fc == 0), stop=(fc == 31))
                s_pm.inc(ins, 1)
            v.wait_ge(s_pm.h, s_pm.v)
            if tg >= 2:
                v.wait_ge(s_rs.h, s_rs.v)
            for th in range(nth):
                v.tensor_add(rf[:], pp2[th][:], b2B[:])
                ins = v.tensor_sub(rb[par][:, th, :], rf[:], Ars[par][:, th, :])
                s_re.inc(ins, 1)
            sy.wait_ge(s_re.h, s_re.v)
            for th in range(nth):
                sy.dma_start(out=self.ap["r_bf_tm"][t0 + 128 * th:
                                                    t0 + 128 * (th + 1), :],
                             in_=rb[par][:, th, :]).then_inc(s_rs.h, 16)
                s_rs.v += 16
            sy.wait_ge(s_fr.h, s_fr.v)
            for th in range(nth):
                sy.dma_start(out=self.ap["a_tm"][t0 + 128 * th:
                                                 t0 + 128 * (th + 1), :],
                             in_=atm[:, th, :]).then_inc(s_rs.h, 16)
                s_rs.v += 16
        sy.wait_ge(s_rs.h, s_rs.v)
        sy.wait_ge(s_hs.h, s_hs.v)

    def p8(self):
        v, s, t, g, sy = self.v, self.s, self.t, self.g, self.sy
        sub = contextlib.ExitStack()
        outer = self.es
        self.es = sub
        w2n, s_l1 = self._load_w_cast("w2n", "w2n")
        db1a = self.sbuf("db1a", [128, 32])
        db2a = self.sbuf("db2a", [128, 8])
        scr = self.sbuf("scr8", [128, 512])
        dbp = self.sbuf("dbp8", [128, 2])
        rfm = self.sbuf("rfm8", [128, 8, 512], BF16)
        rts = [self.sbuf("rt8", [128, E], BF16) for _ in range(2)]
        h1t = [self.sbuf("h1t8", [128, 512], BF16) for _ in range(2)]
        sg8 = self.sbuf("sg8", [128, 512])
        u1 = self.sbuf("u18", [128, 512])
        u2 = self.sbuf("u28", [128, 512])
        q1 = self.sbuf("q18", [128, 512])
        dh1b = [self.sbuf("dh1b", [128, 512], BF16) for _ in range(2)]
        dtm = self.sbuf("dtm8", [128, 4, 4096], BF16)
        pda = [self.psum("pda8", [128, 512]) for _ in range(2)]
        ptA = self.psum("ptA8", [128, 128], BF16)
        ptB = self.psum("ptB8", [128, 128], BF16)
        s_rl = self.sem("rl")
        s_rtp = self.sem("rtp")
        s_rte = self.sem("rte")
        s_dm = self.sem("dm")
        s_ee = self.sem("ee")
        s_hl = self.sem("hl")
        s_dtp = self.sem("dtp")
        s_dfr = self.sem("dfr")
        s_ds = self.sem("ds")
        s_db = self.sem("db")
        dtp_hist = []
        g.memset(db1a[:], 0.0)
        g.memset(db2a[:], 0.0)
        ins = g.nop()
        s_db.inc(ins, 1)
        t.wait_ge(s_l1.h, s_l1.v)
        ntg = 9
        for tg in range(ntg):
            S = 512 if tg < 8 else 128
            t0 = 512 * tg
            nth = S // 128
            for th in range(nth):
                par = th % 2
                if th >= 2:
                    sy.wait_ge(s_rtp.h, s_rtp.v - 8)
                sy.dma_start(out=rts[par][:],
                             in_=self.ap["r_bf_tm"][t0 + 128 * th:
                                                    t0 + 128 * (th + 1), :]) \
                    .then_inc(s_rl.h, 16)
                s_rl.v += 16
                t.wait_ge(s_rl.h, s_rl.v)
                if tg > 0 and th == 0:
                    t.wait_ge(s_ee.h, s_ee.v)
                for c in range(8):
                    pt = (ptA, ptB)[s_rtp.lv % 2]
                    if s_rtp.lv >= 2:
                        t.wait_ge(s_rte.h, s_rte.at(s_rtp.lv - 1))
                    ins = t.transpose(pt[:], rts[par][:, 128 * c:128 * (c + 1)],
                                      self.ident_bf[:])
                    s_rtp.inc(ins, 1)
                    s.wait_ge(s_rtp.h, s_rtp.v)
                    ins = s.copy(rfm[:, c, 128 * th:128 * (th + 1)], pt[:])
                    s_rte.inc(ins, 1)
            s.wait_ge(s_rte.h, s_rte.v)
            for c in range(8):
                s.wait_ge(s_db.h, s_db.v)
                ins = s.activation(scr[:, 0:S], rfm[:, c, 0:S], AF.Identity,
                                   accum_out=dbp[:, 0:1])
                s_db.inc(ins, 1)
                v.wait_ge(s_db.h, s_db.v)
                ins = v.tensor_add(db2a[:, c:c + 1], db2a[:, c:c + 1],
                                   dbp[:, 0:1])
                s_db.inc(ins, 1)
                v.wait_ge(s_db.h, s_db.v)
            for fc in range(32):
                fp = fc % 2
                if tg >= 1 or fc >= 2:
                    sy.wait_ge(s_ee.h, s_ee.at(max(0, s_dm.lv - 1)))
                sy.dma_start(out=h1t[fp][:, 0:S],
                             in_=self.ap["h1_bf"][128 * fc:128 * (fc + 1),
                                                  t0:t0 + S]) \
                    .then_inc(s_hl.h, 16)
                s_hl.v += 16
                if s_dm.lv >= 2:
                    t.wait_ge(s_ee.h, s_ee.at(s_dm.lv - 1))
                for c in range(8):
                    ins = t.matmul(pda[fp][:, 0:S],
                                   w2n[:, c, 128 * fc:128 * (fc + 1)],
                                   rfm[:, c, 0:S], start=(c == 0), stop=(c == 7))
                s_dm.inc(ins, 1)
                s.wait_ge(s_hl.h, s_hl.v)
                s.wait_ge(s_dm.h, s_dm.v)
                if s_ee.lv >= 2:
                    s.wait_ge(s_dtp.h, dtp_hist[-2] if len(dtp_hist) >= 2
                              else 0)
                s.wait_ge(s_ee.h, s_ee.v)
                ins = s.activation(sg8[:, 0:S], h1t[fp][:, 0:S], AF.Sigmoid)
                s_db.inc(ins, 1)
                v.wait_ge(s_db.h, s_db.v)
                v.tensor_scalar(u1[:, 0:S], sg8[:, 0:S], -1.0, 1.0, ALU.mult,
                                ALU.add)
                ins = v.tensor_tensor(u2[:, 0:S], u1[:, 0:S], h1t[fp][:, 0:S],
                                      ALU.mult)
                s_db.inc(ins, 1)
                v.wait_ge(s_db.h, s_db.v)
                v.tensor_scalar_add(u2[:, 0:S], u2[:, 0:S], 1.0)
                ins = v.tensor_mul(q1[:, 0:S], sg8[:, 0:S], u2[:, 0:S])
                s_db.inc(ins, 1)
                v.wait_ge(s_db.h, s_db.v)
                ins = v.tensor_mul(dh1b[fp][:, 0:S], pda[fp][:, 0:S],
                                   q1[:, 0:S])
                s_ee.inc(ins, 1)
                s.wait_ge(s_ee.h, s_ee.v)
                s.wait_ge(s_db.h, s_db.v)
                ins = s.activation(scr[:, 0:S], dh1b[fp][:, 0:S], AF.Identity,
                                   accum_out=dbp[:, 1:2])
                s_db.inc(ins, 1)
                v.wait_ge(s_db.h, s_db.v)
                ins = v.tensor_add(db1a[:, fc:fc + 1], db1a[:, fc:fc + 1],
                                   dbp[:, 1:2])
                s_db.inc(ins, 1)
                v.wait_ge(s_db.h, s_db.v)
                t.wait_ge(s_ee.h, s_ee.v)
                for th in range(nth):
                    pt = (ptA, ptB)[s_dtp.lv % 2]
                    if s_dtp.lv >= 2:
                        t.wait_ge(s_dfr.h, s_dfr.at(s_dtp.lv - 1))
                    ins = t.transpose(pt[:],
                                      dh1b[fp][:, 128 * th:128 * (th + 1)],
                                      self.ident_bf[:])
                    s_dtp.inc(ins, 1)
                    s.wait_ge(s_dtp.h, s_dtp.v)
                    if fc == 0:
                        s.wait_ge(s_ds.h, s_ds.v)
                    ins = s.copy(dtm[:, th, 128 * fc:128 * (fc + 1)], pt[:])
                    s_dfr.inc(ins, 1)
                dtp_hist.append(s_dtp.v)
            sy.wait_ge(s_dfr.h, s_dfr.v)
            for th in range(nth):
                sy.dma_start(out=self.ap["dh1_tm"][t0 + 128 * th:
                                                   t0 + 128 * (th + 1), :],
                             in_=dtm[:, th, :]).then_inc(s_ds.h, 16)
                s_ds.v += 16
        sy.wait_ge(s_ds.h, s_ds.v)
        s_fin = self.sem("fin")
        v.wait_ge(s_db.h, s_db.v)
        ins = v.nop()
        s_fin.inc(ins, 1)
        t.wait_ge(s_fin.h, s_fin.v)
        pfin = self.psum("pfin", [128, 128])
        dbt = self.sbuf("dbt", [32, 128])
        db2t = self.sbuf("db2t", [8, 128])
        ins = t.transpose(pfin[0:32, :], db1a[:], self.ident_f[:])
        s_fin.inc(ins, 1)
        s.wait_ge(s_fin.h, s_fin.v)
        ins = s.copy(dbt[:], pfin[0:32, :])
        s_fin.inc(ins, 1)
        t.wait_ge(s_fin.h, s_fin.v)
        ins = t.transpose(pfin[0:8, :], db2a[:], self.ident_f[:])
        s_fin.inc(ins, 1)
        s.wait_ge(s_fin.h, s_fin.v)
        ins = s.copy(db2t[:], pfin[0:8, :])
        s_fin.inc(ins, 1)
        sy.wait_ge(s_fin.h, s_fin.v)
        gA = self.ap["gradA"]
        gB = self.ap["gradB"]
        sy.dma_start(out=gA[1024:1025, :].rearrange("r (c p) -> (r c) p", p=128),
                     in_=dbt[:]).then_inc(s_fin.h, 16)
        sy.dma_start(out=gB[4096:4097, :].rearrange("r (c p) -> (r c) p", p=128),
                     in_=db2t[:]).then_inc(s_fin.h, 16)
        s_fin.v += 32
        sy.wait_ge(s_fin.h, s_fin.v)
        self.barrier()
        self.es = outer
        sub.close()
        sub = contextlib.ExitStack()
        self.es = sub
        xint = [self.sbuf("xint", [128, E], BF16) for _ in range(2)]
        dht = [self.sbuf("dht", [128, 512], BF16) for _ in range(2)]
        wev = self.sbuf("wev", [128, 512])
        pw = [self.psum("pw8", [128, 512]) for _ in range(8)]
        s_wl = self.sem("wl")
        s_wm = self.sem("wm")
        s_we = self.sem("we")
        s_ws = self.sem("ws")
        ntt = TCP // 128
        for f8 in range(8):
            for tt in range(ntt):
                par = tt % 2
                if s_wl.lv >= 64:
                    sy.wait_ge(s_wm.h, s_wm.v - 8)
                sy.dma_start(out=xint[par][:],
                             in_=self.ap["xin_tm"][128 * tt:128 * (tt + 1), :]) \
                    .then_inc(s_wl.h, 16)
                sy.dma_start(out=dht[par][:],
                             in_=self.ap["dh1_tm"][128 * tt:128 * (tt + 1),
                                                   512 * f8:512 * (f8 + 1)]) \
                    .then_inc(s_wl.h, 16)
                s_wl.v += 32
                t.wait_ge(s_wl.h, s_wl.v)
                if tt == 0:
                    t.wait_ge(s_we.h, s_we.v)
                for dc in range(8):
                    ins = t.matmul(pw[dc][:], xint[par][:, 128 * dc:128 * (dc + 1)],
                                   dht[par][:], start=(tt == 0),
                                   stop=(tt == ntt - 1))
                    s_wm.inc(ins, 1)
            for dc in range(8):
                v.wait_ge(s_wm.h, s_wm.v)
                if s_we.lv >= 1:
                    v.wait_ge(s_ws.h, s_ws.at(16 * s_we.lv))
                ins = v.tensor_copy(wev[:], pw[dc][:])
                s_we.inc(ins, 1)
                sy.wait_ge(s_we.h, s_we.v)
                sy.dma_start(out=self.ap["gradA"][128 * dc:128 * (dc + 1),
                                                  512 * f8:512 * (f8 + 1)],
                             in_=wev[:]).then_inc(s_ws.h, 16)
                s_ws.v += 16
        sy.wait_ge(s_ws.h, s_ws.v)
        self.barrier()
        self.es = outer
        sub.close()
        sub = contextlib.ExitStack()
        self.es = sub
        ats = [self.sbuf("ats", [128, 512], BF16) for _ in range(2)]
        rbs = [self.sbuf("rbs", [128, E], BF16) for _ in range(2)]
        wev2 = self.sbuf("wev2", [128, E])
        pw2 = [self.psum("pw28", [128, 1024]) for _ in range(4)]
        s_wl2 = self.sem("wl2")
        s_wm2 = self.sem("wm2")
        s_we2 = self.sem("we2")
        s_ws2 = self.sem("ws2")
        for fg in range(8):
            for tt in range(ntt):
                par = tt % 2
                if s_wl2.lv >= 64:
                    sy.wait_ge(s_wm2.h, s_wm2.v - 4)
                sy.dma_start(out=ats[par][:],
                             in_=self.ap["a_tm"][128 * tt:128 * (tt + 1),
                                                 512 * fg:512 * (fg + 1)]) \
                    .then_inc(s_wl2.h, 16)
                sy.dma_start(out=rbs[par][:],
                             in_=self.ap["r_bf_tm"][128 * tt:128 * (tt + 1), :]) \
                    .then_inc(s_wl2.h, 16)
                s_wl2.v += 32
                t.wait_ge(s_wl2.h, s_wl2.v)
                if tt == 0:
                    t.wait_ge(s_we2.h, s_we2.v)
                for fc in range(4):
                    for dh in range(2):
                        ins = t.matmul(pw2[fc][:, 512 * dh:512 * (dh + 1)],
                                       ats[par][:, 128 * fc:128 * (fc + 1)],
                                       rbs[par][:, 512 * dh:512 * (dh + 1)],
                                       start=(tt == 0), stop=(tt == ntt - 1))
                    s_wm2.inc(ins, 1)
            for fc in range(4):
                v.wait_ge(s_wm2.h, s_wm2.v)
                if s_we2.lv >= 1:
                    v.wait_ge(s_ws2.h, s_ws2.at(16 * s_we2.lv))
                ins = v.tensor_copy(wev2[:], pw2[fc][:])
                s_we2.inc(ins, 1)
                sy.wait_ge(s_we2.h, s_we2.v)
                r0 = 512 * fg + 128 * fc
                sy.dma_start(out=self.ap["gradB"][r0:r0 + 128, :],
                             in_=wev2[:]).then_inc(s_ws2.h, 16)
                s_ws2.v += 16
        sy.wait_ge(s_ws2.h, s_ws2.v)
        self.barrier()
        self.es = outer
        sub.close()

    def p9(self):
        g, sy = self.g, self.sy
        nc = self.nc
        binA = nc.dram_tensor("binA", [1025, 4096], F32).ap()
        boutA = nc.dram_tensor("boutA", [1025, 4096], F32).ap()
        binB = nc.dram_tensor("binB", [4097, 1024], F32).ap()
        boutB = nc.dram_tensor("boutB", [4097, 1024], F32).ap()
        s_c = self.sem("cc")
        g.dma_start(out=binA[:, :], in_=self.ap["gradA"][:, :]).then_inc(s_c.h, 16)
        g.dma_start(out=binB[:, :], in_=self.ap["gradB"][:, :]).then_inc(s_c.h, 16)
        s_c.v += 32
        g.wait_ge(s_c.h, s_c.v)
        ins = g.collective_compute("AllReduce", ALU.add,
                                   replica_groups=[list(range(8))],
                                   ins=[binA.opt()], outs=[boutA.opt()])
        s_c.inc(ins, 1)
        ins = g.collective_compute("AllReduce", ALU.add,
                                   replica_groups=[list(range(8))],
                                   ins=[binB.opt()], outs=[boutB.opt()])
        s_c.inc(ins, 1)
        g.wait_ge(s_c.h, s_c.v)
        g.dma_start(out=self.ap["gradAr"][:, :], in_=boutA[:, :]).then_inc(s_c.h, 16)
        g.dma_start(out=self.ap["gradBr"][:, :], in_=boutB[:, :]).then_inc(s_c.h, 16)
        s_c.v += 32
        g.wait_ge(s_c.h, s_c.v)

    def p10(self):
        v, s, t, g, sy = self.v, self.s, self.t, self.g, self.sy
        w1, s_l1 = self._load_w_cast("fW1", "fW1")
        w2 = self.sbuf("fW2b", [128, 32, 1024], BF16)
        s_l2 = self.sem("fw2l")
        for c in range(32):
            g.dma_start(out=w2[:, c, :],
                        in_=self.ap["fW2"][128 * c:128 * (c + 1), :]) \
                .then_inc(s_l2.h, 16)
            s_l2.v += 16
        cons = self.sbuf("cons10", [128, 32 * 6])
        s_lc = self.sem("lc")
        sy.dma_start(out=cons[:, 0:32], in_=self.ap["fb1c"][:, :]).then_inc(s_lc.h, 16)
        sy.dma_start(out=cons[:, 32:64], in_=self.ap["phc"][:, :]).then_inc(s_lc.h, 16)
        sy.dma_start(out=cons[:, 64:96], in_=self.ap["wrc"][:, :]).then_inc(s_lc.h, 16)
        sy.dma_start(out=cons[:, 96:128], in_=self.ap["wic"][:, :]).then_inc(s_lc.h, 16)
        s_lc.v += 64
        b2B = self.sbuf("fb2B", [128, E])
        g.dma_start(out=b2B[:], in_=self.ap["fb2row"].partition_broadcast(128)) \
            .then_inc(s_lc.h, 16)
        s_lc.v += 16
        ph = cons[:, 32:64]
        wr_ = cons[:, 64:96]
        wi_ = cons[:, 96:128]
        c1 = cons[:, 128:160]
        tmp1 = cons[:, 160:192]
        s_cc = self.sem("ccm")

        def vfence(ins):
            s_cc.inc(ins, 1)
            v.wait_ge(s_cc.h, s_cc.v)

        v.wait_ge(s_lc.h, s_lc.v)
        s.wait_ge(s_lc.h, s_lc.v)
        vfence(v.tensor_scalar(tmp1[:], ph, 1.0 / TWO_PI, RMAGIC, ALU.mult,
                               ALU.add))
        vfence(v.tensor_scalar(tmp1[:], tmp1[:], 1.0, -RMAGIC, ALU.mult,
                               ALU.add))
        vfence(v.scalar_tensor_tensor(tmp1[:], tmp1[:], -TWO_PI, ph, ALU.mult,
                                      ALU.add))
        s.wait_ge(s_cc.h, s_cc.v)
        ins = s.activation(c1[:], tmp1[:], AF.Sin)
        s_cc.inc(ins, 1)
        v.wait_ge(s_cc.h, s_cc.v)
        vfence(v.tensor_mul(c1[:], c1[:], wi_))
        vfence(v.tensor_scalar(tmp1[:], ph, 1.0, TWO_PI / 4.0, ALU.mult,
                               ALU.add))
        vfence(v.tensor_scalar(cons[:, 192:192] if False else tmp1[:],
                               tmp1[:], 1.0, 0.0, ALU.mult, ALU.add))
        phs = cons[:, 160:192]
        vfence(v.tensor_scalar(cons[:, 128:128] if False else phs,
                               phs, 1.0, 0.0, ALU.mult, ALU.add))
        k2 = self.sbuf("k210", [128, 32])
        vfence(v.tensor_scalar(k2[:], phs, 1.0 / TWO_PI, RMAGIC, ALU.mult,
                               ALU.add))
        vfence(v.tensor_scalar(k2[:], k2[:], 1.0, -RMAGIC, ALU.mult, ALU.add))
        vfence(v.scalar_tensor_tensor(phs, k2[:], -TWO_PI, phs, ALU.mult,
                                      ALU.add))
        s.wait_ge(s_cc.h, s_cc.v)
        cosv = self.sbuf("cos10", [128, 32])
        ins = s.activation(cosv[:], phs, AF.Sin)
        s_cc.inc(ins, 1)
        v.wait_ge(s_cc.h, s_cc.v)
        vfence(v.tensor_mul(cosv[:], cosv[:], wr_))
        vfence(v.tensor_add(c1[:], c1[:], cosv[:]))
        vfence(v.tensor_mul(c1[:], c1[:], wi_))
        vfence(v.tensor_scalar_mul(c1[:], c1[:], 0.1))
        bias2 = self.sbuf("bias2", [128, 32])
        vfence(v.tensor_scalar_mul(bias2[:], cons[:, 0:32], 0.1))
        vfence(v.tensor_add(bias2[:], bias2[:], ph))
        xin = [self.sbuf("xin10", [128, 8, 256], BF16) for _ in range(2)]
        x1t = [self.sbuf("x1t10", [128, 2, E]) for _ in range(2)]
        harg = self.sbuf("harg", [128, 256])
        kk = self.sbuf("kk10", [128, 256])
        sn = self.sbuf("sn10", [128, 256])
        qv = self.sbuf("qv10", [128, 256])
        act = [self.sbuf("act10", [128, 256], BF16) for _ in range(2)]
        xout = [self.sbuf("xo10", [128, 2, E]) for _ in range(2)]
        ph_ = [self.psum("ph10", [128, 256]) for _ in range(2)]
        po = [self.psum("po10", [128, 1024]) for _ in range(2)]
        s_xl = self.sem("xl")
        s_mm = self.sem("mm")
        s_ee = self.sem("ee")
        s_om = self.sem("om")
        s_oe = self.sem("oe")
        s_st = self.sem("st")
        t.wait_ge(s_l1.h, s_l1.v)
        t.wait_ge(s_l2.h, s_l2.v)
        ntg = TC // 256
        for tg in range(ntg):
            par = tg % 2
            t0 = 256 * tg
            seq, off = t0 // N, t0 % N
            col0 = SPAD * seq + PM + off
            if tg >= 2:
                sy.wait_ge(s_mm.h, s_mm.at(32 * (tg - 1)))
            for c in range(8):
                sy.dma_start(out=xin[par][:, c, :],
                             in_=self.ap["xn1_fm"][128 * c:128 * (c + 1),
                                                   col0:col0 + 256]) \
                    .then_inc(s_xl.h, 16)
                s_xl.v += 16
            if tg >= 2:
                sy.wait_ge(s_oe.h, s_oe.at(tg - 1))
            for th in range(2):
                sy.dma_start(out=x1t[par][:, th, :],
                             in_=self.ap["x1"][t0 + 128 * th:
                                               t0 + 128 * (th + 1), :]) \
                    .then_inc(s_xl.h, 16)
                s_xl.v += 16
            t.wait_ge(s_xl.h, s_xl.v)
            for fc in range(32):
                fp = fc % 2
                phh = ph_[fp]
                if s_mm.lv >= 2:
                    t.wait_ge(s_ee.h, s_ee.at(7 * (s_mm.lv - 1)))
                for c in range(8):
                    ins = t.matmul(phh[:], w1[:, c, 128 * fc:128 * (fc + 1)],
                                   xin[par][:, c, :], start=(c == 0),
                                   stop=(c == 7))
                s_mm.inc(ins, 1)
                s.wait_ge(s_mm.h, s_mm.v)
                if s_mm.lv >= 2:
                    s.wait_ge(s_ee.h, s_ee.at(7 * (s_mm.lv - 1)))
                ins = s.activation(harg[:], phh[:], AF.Identity, scale=0.1,
                                   bias=bias2[:, fc:fc + 1])
                s_ee.inc(ins, 1)
                v.wait_ge(s_ee.h, s_ee.v)
                ins = v.tensor_scalar(kk[:], harg[:], 1.0 / TWO_PI, RMAGIC,
                                      ALU.mult, ALU.add)
                s_ee.inc(ins, 1)
                v.wait_ge(s_ee.h, s_ee.v)
                ins = v.tensor_scalar(kk[:], kk[:], 1.0, -RMAGIC, ALU.mult,
                                      ALU.add)
                s_ee.inc(ins, 1)
                v.wait_ge(s_ee.h, s_ee.v)
                ins = v.scalar_tensor_tensor(harg[:], kk[:], -TWO_PI, harg[:],
                                             ALU.mult, ALU.add)
                s_ee.inc(ins, 1)
                s.wait_ge(s_ee.h, s_ee.v)
                ins = s.activation(sn[:], harg[:], AF.Sin)
                s_ee.inc(ins, 1)
                v.wait_ge(s_ee.h, s_ee.v)
                ins = v.scalar_tensor_tensor(qv[:], sn[:],
                                             cons[:, 128 + fc:129 + fc],
                                             phh[:], ALU.mult, ALU.add)
                s_ee.inc(ins, 1)
                s.wait_ge(s_ee.h, s_ee.v)
                if s_mm.lv >= 2:
                    s.wait_ge(s_om.h, s_om.at(s_mm.lv - 1))
                ins = s.activation(act[fp][:], qv[:], AF.Relu,
                                   bias=cons[:, 0 + fc:1 + fc])
                s_ee.inc(ins, 1)
                t.wait_ge(s_ee.h, s_ee.v)
                if fc == 0:
                    t.wait_ge(s_oe.h, s_oe.v)
                for th in range(2):
                    for dh in range(2):
                        ins = t.matmul(po[th][:, 512 * dh:512 * (dh + 1)],
                                       act[fp][:, 128 * th:128 * (th + 1)],
                                       w2[:, fc, 512 * dh:512 * (dh + 1)],
                                       start=(fc == 0), stop=(fc == 31))
                s_om.inc(ins, 1)
            v.wait_ge(s_om.h, s_om.v)
            if tg >= 2:
                v.wait_ge(s_st.h, s_st.at(32 * (tg - 1)))
            for th in range(2):
                v.tensor_add(xout[par][:, th, :], po[th][:], b2B[:])
                ins = v.tensor_add(xout[par][:, th, :], xout[par][:, th, :],
                                   x1t[par][:, th, :])
            s_oe.inc(ins, 1)
            sy.wait_ge(s_oe.h, s_oe.v)
            for th in range(2):
                sy.dma_start(out=self.ap["xpf"][t0 + 128 * th:
                                               t0 + 128 * (th + 1), :],
                             in_=xout[par][:, th, :]).then_inc(s_st.h, 16)
                s_st.v += 16
        sy.wait_ge(s_st.h, s_st.v)

    def p11(self):
        v, s, t, g, sy = self.v, self.s, self.t, self.g, self.sy
        w1p = self.sbuf("w1p", [128, 8, 4096], BF16)
        w2p = self.sbuf("w2p", [128, 32, 1024], BF16)
        b1p = self.sbuf("b1p", [128, 32])
        b2pB = self.sbuf("b2pB", [128, E])
        ones1 = self.sbuf("ones1", [1, 128], BF16)
        s_up = self.sem("up")
        s_ul = self.sem("ul")
        ab = self.sbuf("ab", [128, 2])
        g.dma_start(out=ab[:, 0:1], in_=self.ap["alpha"].partition_broadcast(128)) \
            .then_inc(s_ul.h, 16)
        g.dma_start(out=ab[:, 1:2], in_=self.ap["theta"].partition_broadcast(128)) \
            .then_inc(s_ul.h, 16)
        s_ul.v += 32
        amth = self.sbuf("amth", [128, 2])
        v.wait_ge(s_ul.h, s_ul.v)
        ins = v.tensor_scalar(amth[:, 0:1], ab[:, 0:1], -1.0, 1.0, ALU.mult,
                              ALU.add)
        ins = v.tensor_scalar_mul(amth[:, 1:2], ab[:, 1:2], -1.0)
        s_up.inc(ins, 1)
        v.wait_ge(s_up.h, s_up.v)
        sub = contextlib.ExitStack()
        outer = self.es
        self.es = sub
        wld = self.sbuf("wld", [128, 4096])
        gld = self.sbuf("gld", [128, 4096])
        tt1 = self.sbuf("tt1", [128, 4096])
        s_wl = self.sem("wl")
        s_wc = self.sem("wc")
        for dc in range(8):
            if dc >= 1:
                sy.wait_ge(s_wc.h, s_wc.at(2 * dc))
            sy.dma_start(out=wld[:], in_=self.ap["w1T"][128 * dc:128 * (dc + 1), :]) \
                .then_inc(s_wl.h, 16)
            sy.dma_start(out=gld[:],
                         in_=self.ap["gradAr"][128 * dc:128 * (dc + 1), :]) \
                .then_inc(s_wl.h, 16)
            s_wl.v += 32
            v.wait_ge(s_wl.h, s_wl.v)
            ins = v.tensor_scalar_mul(tt1[:], wld[:], amth[:, 0:1])
            s_wc.inc(ins, 1)
            v.wait_ge(s_wc.h, s_wc.v)
            ins = v.scalar_tensor_tensor(w1p[:, dc, :], gld[:],
                                         amth[:, 1:2], tt1[:], ALU.mult,
                                         ALU.add)
            s_wc.inc(ins, 1)
        for c in range(32):
            sy.wait_ge(s_wc.h, s_wc.at(16 + 2 * c))
            sy.dma_start(out=wld[:, 0:1024],
                         in_=self.ap["w2T"][128 * c:128 * (c + 1), :]) \
                .then_inc(s_wl.h, 16)
            sy.dma_start(out=gld[:, 0:1024],
                         in_=self.ap["gradBr"][128 * c:128 * (c + 1), :]) \
                .then_inc(s_wl.h, 16)
            s_wl.v += 32
            v.wait_ge(s_wl.h, s_wl.v)
            ins = v.tensor_scalar_mul(tt1[:, 0:1024], wld[:, 0:1024],
                                      amth[:, 0:1])
            s_wc.inc(ins, 1)
            v.wait_ge(s_wc.h, s_wc.v)
            ins = v.scalar_tensor_tensor(w2p[:, c, :], gld[:, 0:1024],
                                         amth[:, 1:2], tt1[:, 0:1024],
                                         ALU.mult, ALU.add)
            s_wc.inc(ins, 1)
        b1cs = self.sbuf("b1cs", [128, 32])
        db1s = self.sbuf("db1s", [32, 128])
        db1c = self.sbuf("db1c", [128, 32])
        b2r = self.sbuf("b2r", [1, E])
        db2r = self.sbuf("db2r", [1, E])
        b2p1 = self.sbuf("b2p1", [1, E], BF16)
        pfin = self.psum("pfin11", [128, 128])
        pbb = self.psum("pbb11", [128, 512])
        sy.dma_start(out=b1cs[:], in_=self.ap["b1c"][:, :]).then_inc(s_wl.h, 16)
        sy.dma_start(out=db1s[:],
                     in_=self.ap["gradAr"][1024:1025, :]
                     .rearrange("r (c p) -> (r c) p", p=128)) \
            .then_inc(s_wl.h, 16)
        sy.dma_start(out=b2r[:], in_=self.ap["b2row"][:, :]).then_inc(s_wl.h, 16)
        sy.dma_start(out=db2r[:], in_=self.ap["gradBr"][4096:4097, :]) \
            .then_inc(s_wl.h, 16)
        s_wl.v += 64
        t.wait_ge(s_wl.h, s_wl.v)
        s_fin = self.sem("f11")
        ins = t.transpose(pfin[:, 0:32], db1s[:], self.ident_f[0:32, 0:32])
        s_fin.inc(ins, 1)
        v.wait_ge(s_fin.h, s_fin.v)
        ins = v.tensor_copy(db1c[:], pfin[:, 0:32])
        s_fin.inc(ins, 1)
        v.wait_ge(s_fin.h, s_fin.v)
        ins = v.tensor_scalar_mul(b1p[:], b1cs[:], amth[:, 0:1])
        s_fin.inc(ins, 1)
        v.wait_ge(s_fin.h, s_fin.v)
        ins = v.scalar_tensor_tensor(b1p[:], db1c[:], amth[:, 1:2], b1p[:],
                                     ALU.mult, ALU.add)
        s_fin.inc(ins, 1)
        v.wait_ge(s_fin.h, s_fin.v)
        ins = v.tensor_scalar_mul(b2r[:], b2r[:], amth[0:1, 0:1])
        s_fin.inc(ins, 1)
        v.wait_ge(s_fin.h, s_fin.v)
        ins = v.scalar_tensor_tensor(b2p1[:], db2r[:], amth[0:1, 1:2], b2r[:],
                                     ALU.mult, ALU.add)
        s_fin.inc(ins, 1)
        g.memset(ones1[:], 1.0)
        ins = g.nop()
        s_fin.inc(ins, 1)
        t.wait_ge(s_fin.h, s_fin.v)
        for hlf in range(2):
            ins = t.matmul(pbb[:], ones1[:], b2p1[:, 512 * hlf:512 * (hlf + 1)],
                           start=True, stop=True)
            s_fin.inc(ins, 1)
            v.wait_ge(s_fin.h, s_fin.v)
            ins = v.tensor_copy(b2pB[:, 512 * hlf:512 * (hlf + 1)], pbb[:])
            s_fin.inc(ins, 1)
            t.wait_ge(s_fin.h, s_fin.v)
        v.wait_ge(s_wc.h, s_wc.v)
        ins = v.nop()
        s_fin.inc(ins, 1)
        self.barrier()
        self.es = outer
        sub.close()
        sub = contextlib.ExitStack()
        self.es = sub
        xin = [self.sbuf("xin11", [128, 8, 256], BF16) for _ in range(2)]
        a2t = [self.sbuf("a2t", [128, 256], BF16) for _ in range(2)]
        omo = [self.sbuf("omo", [128, 2, E]) for _ in range(2)]
        ph_ = [self.psum("ph11", [128, 256]) for _ in range(2)]
        po = [self.psum("po11", [128, 1024]) for _ in range(2)]
        s_xl = self.sem("xl")
        s_mm = self.sem("mm")
        s_ev = self.sem("ev")
        s_om = self.sem("om")
        s_oe = self.sem("oe")
        s_st = self.sem("st")
        ntg = TCP // 256 + 1
        for tg in range(ntg):
            S = 256 if tg < 16 else 128
            nth = S // 128
            t0 = 256 * tg
            par = tg % 2
            if tg >= 2:
                sy.wait_ge(s_mm.h, s_mm.at(32 * (tg - 1)))
            for c in range(8):
                sy.dma_start(out=xin[par][:, c, 0:S],
                             in_=self.ap["xin_fm_c"][128 * c:128 * (c + 1),
                                                     t0:t0 + S]) \
                    .then_inc(s_xl.h, 16)
                s_xl.v += 16
            t.wait_ge(s_xl.h, s_xl.v)
            for fc in range(32):
                fp = fc % 2
                phh = ph_[fp]
                if s_mm.lv >= 2:
                    t.wait_ge(s_ev.h, s_ev.at(s_mm.lv - 1))
                for c in range(8):
                    ins = t.matmul(phh[:, 0:S], w1p[:, c, 128 * fc:128 * (fc + 1)],
                                   xin[par][:, c, 0:S], start=(c == 0),
                                   stop=(c == 7))
                s_mm.inc(ins, 1)
                s.wait_ge(s_mm.h, s_mm.v)
                if s_ev.lv >= 2:
                    s.wait_ge(s_om.h, s_om.at(s_ev.lv - 1))
                ins = s.activation(a2t[fp][:, 0:S], phh[:, 0:S], AF.Silu,
                                   bias=b1p[:, fc:fc + 1])
                s_ev.inc(ins, 1)
                t.wait_ge(s_ev.h, s_ev.v)
                if fc == 0:
                    t.wait_ge(s_oe.h, s_oe.v)
                for th in range(nth):
                    for dh in range(2):
                        ins = t.matmul(po[th][:, 512 * dh:512 * (dh + 1)],
                                       a2t[fp][:, 128 * th:128 * (th + 1)],
                                       w2p[:, fc, 512 * dh:512 * (dh + 1)],
                                       start=(fc == 0), stop=(fc == 31))
                s_om.inc(ins, 1)
            v.wait_ge(s_om.h, s_om.v)
            if tg >= 2:
                v.wait_ge(s_st.h, s_st.v - 16 * nth)
            for th in range(nth):
                ins = v.tensor_add(omo[par][:, th, :], po[th][:], b2pB[:])
            s_oe.inc(ins, 1)
            sy.wait_ge(s_oe.h, s_oe.v)
            for th in range(nth):
                sy.dma_start(out=self.ap["om_tm"][t0 + 128 * th:
                                                  t0 + 128 * (th + 1), :],
                             in_=omo[par][:, th, :]).then_inc(s_st.h, 16)
                s_st.v += 16
        sy.wait_ge(s_st.h, s_st.v)
        self.barrier()
        self.es = outer
        sub.close()

    def p12(self):
        v, s, t, g, sy = self.v, self.s, self.t, self.g, self.sy
        l1, s_l1 = self._load_w_cast("lin1T", "l1T")
        l2, s_l2 = self._load_w_cast("lin2T", "l2T")
        At = [self.sbuf("At12", [128, E]) for _ in range(2)]
        Om = [self.sbuf("Om12", [128, E]) for _ in range(2)]
        Ab = self.sbuf("Ab12", [128, E], BF16)
        Ob = self.sbuf("Ob12", [128, E], BF16)
        Af = self.sbuf("Af12", [128, 8, 128], BF16)
        Of = self.sbuf("Of12", [128, 8, 128], BF16)
        g1s = self.sbuf("g1s", [128, E])
        g2s = self.sbuf("g2s", [128, E])
        mg1 = self.sbuf("mg1", [128, E])
        mg2 = self.sbuf("mg2", [128, E])
        mrg = [self.sbuf("mrg", [128, E]) for _ in range(2)]
        xpf = [self.sbuf("xpf12", [128, E]) for _ in range(2)]
        yt = [self.sbuf("yt12", [128, E]) for _ in range(2)]
        pg1 = self.psum("pg1_12", [128, 1024])
        pg2 = self.psum("pg2_12", [128, 1024])
        ptA = self.psum("ptA12", [128, 128], BF16)
        ptB = self.psum("ptB12", [128, 128], BF16)
        s_ld = self.sem("ld")
        s_cb = self.sem("cb")
        s_tp = self.sem("tp")
        s_fr = self.sem("fr")
        s_gm = self.sem("gm")
        s_ge = self.sem("ge")
        s_mr = self.sem("mr")
        s_st = self.sem("st")
        t.wait_ge(s_l1.h, s_l1.v)
        t.wait_ge(s_l2.h, s_l2.v)
        for i in range(TCP // 128):
            par = i % 2
            seq = i // (SP // 64) if False else (0 if i < 17 else 1)
            r0 = 128 * i
            nrow = 128
            if i >= 2:
                sy.wait_ge(s_mr.h, s_mr.at(3 * (i - 1)))
            sy.dma_start(out=At[par][:], in_=self.ap["A_tm"][r0:r0 + 128, :]) \
                .then_inc(s_ld.h, 16)
            sy.dma_start(out=Om[par][:], in_=self.ap["om_tm"][r0:r0 + 128, :]) \
                .then_inc(s_ld.h, 16)
            s_ld.v += 32
            v.wait_ge(s_ld.h, s_ld.v)
            if i >= 1:
                v.wait_ge(s_tp.h, s_tp.at(16 * i))
            v.tensor_copy(Ab[:], At[par][:])
            ins = v.tensor_copy(Ob[:], Om[par][:])
            s_cb.inc(ins, 1)
            t.wait_ge(s_cb.h, s_cb.v)
            for src, dstf in ((Ab, Af), (Ob, Of)):
                for c in range(8):
                    pt = (ptA, ptB)[s_tp.lv % 2]
                    if s_tp.lv >= 2:
                        t.wait_ge(s_fr.h, s_fr.at(s_tp.lv - 1))
                    ins = t.transpose(pt[:], src[:, 128 * c:128 * (c + 1)],
                                      self.ident_bf[:])
                    s_tp.inc(ins, 1)
                    s.wait_ge(s_tp.h, s_tp.v)
                    if i >= 1 and c == 0 and src is Ab:
                        s.wait_ge(s_gm.h, s_gm.v)
                    ins = s.copy(dstf[:, c, :], pt[:])
                    s_fr.inc(ins, 1)
            t.wait_ge(s_fr.h, s_fr.v)
            if i >= 1:
                t.wait_ge(s_ge.h, s_ge.at(2 * i))
            for c in range(8):
                ins = t.matmul(pg1[:, 0:512], Af[:, c, :], l1[:, c, 0:512],
                               start=(c == 0), stop=(c == 7))
                ins = t.matmul(pg1[:, 512:1024], Af[:, c, :], l1[:, c, 512:1024],
                               start=(c == 0), stop=(c == 7))
                ins = t.matmul(pg2[:, 0:512], Of[:, c, :], l2[:, c, 0:512],
                               start=(c == 0), stop=(c == 7))
                ins = t.matmul(pg2[:, 512:1024], Of[:, c, :], l2[:, c, 512:1024],
                               start=(c == 0), stop=(c == 7))
            s_gm.inc(ins, 1)
            s.wait_ge(s_gm.h, s_gm.v)
            if i >= 2:
                s.wait_ge(s_mr.h, s_mr.at(3 * (i - 1)))
            s.activation(g1s[:, 0:512], pg1[:, 0:512], AF.Sigmoid)
            ins = s.activation(g1s[:, 512:1024], pg1[:, 512:1024], AF.Sigmoid)
            s.activation(g2s[:, 0:512], pg2[:, 0:512], AF.Sigmoid)
            ins = s.activation(g2s[:, 512:1024], pg2[:, 512:1024], AF.Sigmoid)
            s_ge.inc(ins, 2)
            v.wait_ge(s_ge.h, s_ge.v)
            v.tensor_mul(mg1[:], g1s[:], At[par][:])
            g.wait_ge(s_ge.h, s_ge.v)
            g.wait_ge(s_ld.h, s_ld.v)
            if i >= 1:
                g.wait_ge(s_mr.h, s_mr.at(3 * i - 1))
            ins = g.tensor_mul(mg2[:], g2s[:], Om[par][:])
            s_mr.inc(ins, 1)
            v.wait_ge(s_mr.h, s_mr.v)
            if i >= 2:
                v.wait_ge(s_st.h, s_st.at(16 * (i - 1)))
            ins = v.tensor_add(mrg[par][:], mg1[:], mg2[:])
            s_mr.inc(ins, 1)
            if i == 0:
                mrow0, mrows, srow0 = 0, 64, 64
            elif i <= 15:
                mrow0, mrows, srow0 = 128 * i - 64, 128, 0
            elif i == 16:
                mrow0, mrows, srow0 = 1984, 64, 0
            else:
                mrow0, mrows, srow0 = 128 * (i - 1), 128, 0
            sy.dma_start(out=xpf[par][srow0:srow0 + mrows, :],
                         in_=self.ap["xpf"][mrow0:mrow0 + mrows, :]) \
                .then_inc(s_ld.h, 16)
            s_ld.v += 16
            v.wait_ge(s_ld.h, s_ld.v)
            ins = v.tensor_add(yt[par][srow0:srow0 + mrows, :],
                               mrg[par][srow0:srow0 + mrows, :],
                               xpf[par][srow0:srow0 + mrows, :])
            s_mr.inc(ins, 1)
            sy.wait_ge(s_mr.h, s_mr.v)
            sy.dma_start(out=self.ap["y"][mrow0:mrow0 + mrows, :],
                         in_=yt[par][srow0:srow0 + mrows, :]).then_inc(s_st.h, 16)
            s_st.v += 16
        sy.wait_ge(s_st.h, s_st.v)

def _prep(inputs):
    c = np.ascontiguousarray
    f = np.float32
    d = {}
    x = np.asarray(inputs["L0_attn_x"], f)
    d["__x_full"] = x
    inv = (10000.0 ** (-(np.arange(32, dtype=np.float64) * 2.0) / 64.0))
    pos = np.arange(N, dtype=np.float64)
    ang = pos[:, None] * inv[None, :]
    cos = np.cos(ang).astype(f)
    sin = np.sin(ang).astype(f)
    cosF = np.tile(np.concatenate([cos, cos], 1), (1, 16))
    sinF = np.tile(np.concatenate([-sin, sin], 1), (1, 16))
    d["cosF"] = c(cosF)
    d["sinF"] = c(sinF)
    d["wq"] = c(np.asarray(inputs["L0_attn_Wq"], f))
    d["wk"] = c(np.asarray(inputs["L0_attn_Wk"], f))
    d["wv"] = c(np.asarray(inputs["L0_attn_Wv"], f))
    d["wo"] = c(np.asarray(inputs["L0_attn_Wo"], f))
    d["wg1"] = c(np.asarray(inputs["L0_attn_Wg1"], f))
    d["wg2"] = c(np.asarray(inputs["L0_attn_Wg2"], f))
    d["gamma"] = c(np.asarray(inputs["L0_attn_gamma"], f)[None, :])
    d["beta"] = c(np.asarray(inputs["L0_attn_beta"], f)[None, :])
    d["gwT"] = c(np.asarray(inputs["L0_ffn_gate_weight"], f).T)
    d["gbias"] = c(np.asarray(inputs["L0_ffn_gate_bias"], f)[None, :])
    d["mw_g"] = c(np.asarray(inputs["L0_ffn_w_gate"], f).reshape(32, 8, 128, 512))
    d["mw_u"] = c(np.asarray(inputs["L0_ffn_w_up"], f).reshape(32, 8, 128, 512))
    d["mw_d"] = c(np.asarray(inputs["L0_ffn_w_down"], f).reshape(32, 4, 128, 1024))
    d["sw_g"] = c(np.asarray(inputs["L0_ffn_shared_w_gate"], f).reshape(8, 128, 512))
    d["sw_u"] = c(np.asarray(inputs["L0_ffn_shared_w_up"], f).reshape(8, 128, 512))
    d["sw_d"] = c(np.asarray(inputs["L0_ffn_shared_w_down"], f).reshape(4, 128, 1024))
    d["aw_g"] = c(np.asarray(inputs["L0_ffn_anchor_w_gate"], f).reshape(16, 8, 128, 256))
    d["aw_u"] = c(np.asarray(inputs["L0_ffn_anchor_w_up"], f).reshape(16, 8, 128, 256))
    d["aw_d"] = c(np.asarray(inputs["L0_ffn_anchor_w_down"], f).reshape(16, 2, 128, 1024))
    d["mem"] = c(np.asarray(inputs["L1_attn_mem"], f))
    d["l1qT"] = c(np.asarray(inputs["L1_attn_wq"], f).T)
    d["l1kT"] = c(np.asarray(inputs["L1_attn_wk"], f).T)
    d["l1vT"] = c(np.asarray(inputs["L1_attn_wv"], f).T)
    d["l1oT"] = c(np.asarray(inputs["L1_attn_wo"], f).T)
    d["lin1T"] = c(np.asarray(inputs["L1_attn_lin1"], f).T)
    d["lin2T"] = c(np.asarray(inputs["L1_attn_lin2"], f).T)
    d["w1T"] = c(np.asarray(inputs["L1_attn_w1"], f).T)
    d["w2T"] = c(np.asarray(inputs["L1_attn_w2"], f).T)
    d["w2n"] = c(np.asarray(inputs["L1_attn_w2"], f))
    d["b1c"] = c(np.asarray(inputs["L1_attn_b1"], f).reshape(32, 128).T)
    d["b2row"] = c(np.asarray(inputs["L1_attn_b2"], f)[None, :])
    d["alpha"] = c(np.asarray(inputs["L1_attn_alpha"], f).reshape(1, 1))
    d["theta"] = c(np.asarray(inputs["L1_attn_theta"], f).reshape(1, 1))
    d["fW1"] = c(np.asarray(inputs["L1_ffn_W1"], f))
    d["fW2"] = c(np.asarray(inputs["L1_ffn_W2"], f))
    d["fb1c"] = c(np.asarray(inputs["L1_ffn_b1"], f).reshape(32, 128).T)
    d["fb2row"] = c(np.asarray(inputs["L1_ffn_b2"], f)[None, :])
    d["phc"] = c(np.asarray(inputs["L1_ffn_quantum_phases"], f).reshape(32, 128).T)
    d["wrc"] = c(np.asarray(inputs["L1_ffn_quantum_weights_real"], f).reshape(32, 128).T)
    d["wic"] = c(np.asarray(inputs["L1_ffn_quantum_weights_imag"], f).reshape(32, 128).T)
    return d


LAUNCH_GROUPS = [ALL_PHASES]


def run_phases(prep, scratch_host, groups, fused=False):
    for grp in groups:
        mk = MK(grp, fused=fused)
        in_maps = []
        for core in range(NCORES):
            m = {}
            for nm in mk.in_names:
                if nm == "x":
                    m["x"] = np.ascontiguousarray(
                        prep["__x_full"][2 * core:2 * core + 2].reshape(TC, E))
                elif nm in PREP_INPUTS:
                    m[nm] = prep[nm]
                else:
                    m[nm] = scratch_host[core][nm]
            in_maps.append(m)
        res = run_bass_kernel_spmd(mk.nc, in_maps, core_ids=list(range(NCORES)))
        for core in range(NCORES):
            for nm in mk.out_names:
                scratch_host[core][nm] = res.results[core][nm]
    return scratch_host


def kernel(**inputs):
    prep = _prep(inputs)
    scratch = [dict() for _ in range(NCORES)]
    run_phases(prep, scratch, LAUNCH_GROUPS, fused=False)
    out = np.empty((16, N, E), np.float32)
    for core in range(NCORES):
        y = scratch[core].get("y")
        out[2 * core:2 * core + 2] = y.reshape(2, N, E)
    return out
